# version 03
# baseline (speedup 1.0000x reference)
import numpy as np
import ml_dtypes
import concourse.bass as bass
import concourse.mybir as mybir
from concourse.bass_utils import run_bass_kernel_spmd
from contextlib import ExitStack

F32 = mybir.dt.float32
BF16 = mybir.dt.bfloat16
AF = mybir.ActivationFunctionType
ALU = mybir.AluOpType
AXX = mybir.AxisListType.X

H = 64; PD = 64; ND = 128; GD = 8; KC = 4
DM = 4096; ID = 4096; GN = 1024; CONV = 6144; PROJ = 10304
L = 8192; NCORE = 8; LC = 1024; Q = 128; NQ = 8
NPT = 81
HBC0, HBC1 = 32, 80
DTT = 80


def build_nc():
    nc = bass.Bass("TRN2", target_bir_lowering=False)
    def din(name, shape, dt=F32):
        return nc.dram_tensor(name, shape, dt, kind="ExternalInput").ap()

    x_loc = din("x_loc", [LC, DM])
    x_halo = din("x_halo", [3, DM])
    xT_loc = din("xT_loc", [DM, LC])
    Win_b = din("Win_b", [NPT, 32, 128, 128])
    Wout_b = din("Wout_b", [32, 32, 128, 128])
    W12_b = din("W12_b", [64, 32, 128, 128])
    W3_b = din("W3_b", [32, 32, 128, 128])
    conv_w = din("conv_w", [CONV, KC])
    conv_b = din("conv_b", [CONV])
    dt_bias = din("dt_bias", [H])
    A_log = din("A_log", [H])
    D_in = din("D_in", [H])
    nw_in = din("nw_in", [ID])
    b12_in = din("b12_in", [2 * DM])
    b3_in = din("b3_in", [DM])
    identb = din("identb", [128, 128], BF16)
    identf = din("identf", [128, 128])
    AM_in = din("AM_in", [128, 128])
    M1_in = din("M1_in", [128, H * NCORE])

    outT = nc.dram_tensor("outT", [DM, LC], F32, kind="ExternalOutput").ap()

    d_gate = nc.dram_tensor("d_gate", [128, 32 * 1024], BF16).ap()
    d_xs = nc.dram_tensor("d_xs", [128, 32 * 1024], BF16).ap()
    d_x4 = nc.dram_tensor("d_x4", [128, 32 * 1024], BF16).ap()
    d_yn = nc.dram_tensor("d_yn", [128, 32 * 1024], BF16).ap()
    cc_in = nc.dram_tensor("cc_in", [128, 4097], F32)
    cc_out = nc.dram_tensor("cc_out", [128 * NCORE, 4097], F32)

    C = {"t": 0, "v": 0, "s": 0, "w": 0, "l": 0, "p": 0, "o": 0, "cc": 0}

    with ExitStack() as ctx:
        E = ctx.enter_context
        arena = E(nc.sbuf_tensor("arena", [128, 32 * 1027], BF16))
        BC = E(nc.sbuf_tensor("BC", [128, 16384], BF16))
        wsb = E(nc.sbuf_tensor("wsb", [128, 8192], BF16))
        f32a = E(nc.sbuf_tensor("f32a", [128, 4096], F32))
        srun = E(nc.sbuf_tensor("srun", [128, 4096], F32))
        sbf = E(nc.sbuf_tensor("sbf", [128, 4096], BF16))
        cum = E(nc.sbuf_tensor("cum", [H, LC], F32))
        dtp = E(nc.sbuf_tensor("dtp", [H, LC], F32))
        dtsb = E(nc.sbuf_tensor("dtsb", [H, LC], F32))
        dt64 = E(nc.sbuf_tensor("dt64", [H, LC], F32))
        cumT = E(nc.sbuf_tensor("cumT", [128, 512], F32))
        dtpT = E(nc.sbuf_tensor("dtpT", [128, 512], F32))
        ends_rep = E(nc.sbuf_tensor("ends_rep", [128, 512], F32))
        dq_rep = E(nc.sbuf_tensor("dq_rep", [128, 512], F32))
        pref_rep = E(nc.sbuf_tensor("pref_rep", [128, 512], F32))
        W8_rep = E(nc.sbuf_tensor("W8_rep", [128, 512], F32))
        D_rep = E(nc.sbuf_tensor("D_rep", [128, 64], F32))
        Drow = E(nc.sbuf_tensor("Drow", [1, 64], F32))
        stage = E(nc.sbuf_tensor("stage", [1, 1024], F32))
        onesf = E(nc.sbuf_tensor("onesf", [1, 128], F32))
        ones64 = E(nc.sbuf_tensor("ones64", [H, 128], F32))
        identb_s = E(nc.sbuf_tensor("identb_s", [128, 128], BF16))
        identf_s = E(nc.sbuf_tensor("identf_s", [128, 128], F32))
        AM = E(nc.sbuf_tensor("AM", [128, 128], F32))
        M1sb = E(nc.sbuf_tensor("M1sb", [128, H * NCORE], F32))
        convw_sb = E(nc.sbuf_tensor("convw_sb", [128, 192], F32))
        convb_sb = E(nc.sbuf_tensor("convb_sb", [128, 48], F32))
        dtb_col = E(nc.sbuf_tensor("dtb_col", [H, 1], F32))
        Alog_col = E(nc.sbuf_tensor("Alog_col", [H, 1], F32))
        Acol = E(nc.sbuf_tensor("Acol", [H, 1], F32))
        nw_sb = E(nc.sbuf_tensor("nw_sb", [128, 32], F32))
        b12_sb = E(nc.sbuf_tensor("b12_sb", [128, 64], F32))
        b3_sb = E(nc.sbuf_tensor("b3_sb", [128, 32], F32))
        ends = E(nc.sbuf_tensor("ends", [H, NQ], F32))
        scanned = E(nc.sbuf_tensor("scanned", [H, NQ], F32))
        pref = E(nc.sbuf_tensor("pref", [H, NQ], F32))
        CEall = E(nc.sbuf_tensor("CEall", [H, NCORE], F32))
        Ssum = E(nc.sbuf_tensor("Ssum", [H, NCORE], F32))
        Sme = E(nc.sbuf_tensor("Sme", [H, 1], F32))
        tmp8 = E(nc.sbuf_tensor("tmp8", [H, NCORE], F32))
        lncol = E(nc.sbuf_tensor("lncol", [128, 8], F32))
        G_bf = E(nc.sbuf_tensor("G_bf", [128, 128], BF16))
        BT_sb = E(nc.sbuf_tensor("BT_sb", [128, 128], BF16))
        Dm = E(nc.sbuf_tensor("Dm", [128, 512], BF16))
        e_sb = E(nc.sbuf_tensor("e_sb", [128, 512], F32))
        Mp = E(nc.sbuf_tensor("Mp", [128, 256], BF16))
        xts = E(nc.sbuf_tensor("xts", [128, 128], BF16))
        dtx = E(nc.sbuf_tensor("dtx", [128, 128], BF16))
        dtxs = E(nc.sbuf_tensor("dtxs", [128, 128], BF16))
        colb = E(nc.sbuf_tensor("colb", [128, 8], F32))
        yg1 = E(nc.sbuf_tensor("yg1", [128, 128], F32))
        yg2 = E(nc.sbuf_tensor("yg2", [128, 128], F32))
        gstage = E(nc.sbuf_tensor("gstage", [128, 1024], BF16))
        xstage = E(nc.sbuf_tensor("xstage", [128, 1024], BF16))
        x4stage = E(nc.sbuf_tensor("x4stage", [128, 1024], BF16))
        silu_t = E(nc.sbuf_tensor("silu_t", [128, 1024], BF16))
        ynstage = E(nc.sbuf_tensor("ynstage", [128, 256], BF16))
        ss_sb = E(nc.sbuf_tensor("ss_sb", [128, 64], F32))
        rg_sb = E(nc.sbuf_tensor("rg_sb", [128, 64], F32))

        pmA = E(nc.psum_tensor("pmA", [128, 512], F32))
        pmB = E(nc.psum_tensor("pmB", [128, 512], F32))
        pmC = E(nc.psum_tensor("pmC", [128, 512], F32))
        pmD = E(nc.psum_tensor("pmD", [128, 512], F32))
        pmS = E(nc.psum_tensor("pmS", [128, 512], F32))
        pbT = E(nc.psum_tensor("pbT", [128, 512], BF16))
        pmH = E(nc.psum_tensor("pmH", [128, 8], F32))

        s_t = E(nc.semaphore("s_t")); s_v = E(nc.semaphore("s_v"))
        s_s = E(nc.semaphore("s_s")); s_w = E(nc.semaphore("s_w"))
        s_l = E(nc.semaphore("s_l")); s_p = E(nc.semaphore("s_p"))
        s_o = E(nc.semaphore("s_o")); s_cc = E(nc.semaphore("s_cc"))
        SEM = {"t": s_t, "v": s_v, "s": s_s, "w": s_w, "l": s_l, "p": s_p, "o": s_o, "cc": s_cc}
        block = E(nc.Block())

        def inc(key, instr, n=1):
            instr.then_inc(SEM[key], n)
            C[key] += n
            return C[key]

        @block.sync
        def _(sy):
            with nc.allow_non_contiguous_dma(reason="small one-time const loads"):
                inc("l", sy.dma_start(out=identb_s[:], in_=identb), 16)
                inc("l", sy.dma_start(out=identf_s[:], in_=identf), 16)
                inc("l", sy.dma_start(out=AM[:], in_=AM_in), 16)
                inc("l", sy.dma_start(out=M1sb[:], in_=M1_in), 16)
                inc("l", sy.dma_start(out=convb_sb[:], in_=conv_b.rearrange("(a j p) -> a p j", a=1, p=128)), 16)
                inc("l", sy.dma_start(out=convw_sb[:].rearrange("p (j k) -> p j k", k=KC),
                                      in_=conv_w.rearrange("(j p) k -> p j k", p=128)), 16)
                inc("l", sy.dma_start(out=dtb_col[:], in_=dt_bias.rearrange("(p b) -> p b", b=1)), 16)
                inc("l", sy.dma_start(out=Alog_col[:], in_=A_log.rearrange("(p b) -> p b", b=1)), 16)
                inc("l", sy.dma_start(out=Drow[:], in_=D_in.rearrange("(a k) -> a k", a=1)), 16)
                inc("l", sy.dma_start(out=nw_sb[:], in_=nw_in.rearrange("(a j p) -> a p j", a=1, p=128)), 16)
                inc("l", sy.dma_start(out=b12_sb[:], in_=b12_in.rearrange("(a j p) -> a p j", a=1, p=128)), 16)
                inc("l", sy.dma_start(out=b3_sb[:], in_=b3_in.rearrange("(a j p) -> a p j", a=1, p=128)), 16)
        const_l = C["l"]

        @block.vector
        def _(v):
            v.wait_ge(s_l, const_l)
            inc("v", v.memset(onesf[:], 1.0))
            inc("v", v.memset(ones64[:], 1.0))

        for j in range(9):
            npart = 128 if j < 8 else 3
            src = x_loc[j * 128:(j + 1) * 128, :] if j < 8 else x_halo
            ld_ms = [0]

            @block.sync
            def _(sy, src=src, npart=npart, ld_ms=ld_ms, j=j):
                if j > 0:
                    sy.wait_ge(s_v, C["v"])
                ld_ms[0] = inc("l", sy.dma_start(out=srun[0:npart, :], in_=src), 16)

            @block.scalar
            def _(s, npart=npart, ld_ms=ld_ms):
                s.wait_ge(s_l, ld_ms[0])
                s.wait_ge(s_v, C["v"])
                inc("s", s.activation(f32a[0:npart, :], srun[0:npart, :], AF.Identity, accum_out=lncol[0:npart, 0:1]))
                inc("s", s.activation(f32a[0:npart, :], srun[0:npart, :], AF.Square, accum_out=lncol[0:npart, 1:2]))
            ssq = C["s"]

            @block.vector
            def _(v, npart=npart, ssq=ssq):
                v.wait_ge(s_s, ssq)
                inc("v", v.tensor_scalar(lncol[0:npart, 2:3], lncol[0:npart, 0:1], 1.0 / DM, None, ALU.mult))
                inc("v", v.tensor_scalar(lncol[0:npart, 4:5], lncol[0:npart, 1:2], 1.0 / DM, 1e-5, ALU.mult, ALU.add))
            vvar = C["v"]

            @block.scalar
            def _(s, npart=npart, vvar=vvar):
                s.wait_ge(s_v, vvar)
                inc("s", s.activation(lncol[0:npart, 3:4], lncol[0:npart, 2:3], AF.Identity, scale=-1.0))
            sneg = C["s"]

            @block.vector
            def _(v, npart=npart, sneg=sneg):
                v.wait_ge(s_s, sneg)
                inc("v", v.scalar_tensor_tensor(lncol[0:npart, 5:6], lncol[0:npart, 2:3], lncol[0:npart, 3:4],
                                                lncol[0:npart, 4:5], ALU.mult, ALU.add))
            vve = C["v"]

            @block.scalar
            def _(s, npart=npart, vve=vve):
                s.wait_ge(s_v, vve)
                inc("s", s.activation(lncol[0:npart, 6:7], lncol[0:npart, 5:6], AF.Ln))
                inc("s", s.activation(lncol[0:npart, 6:7], lncol[0:npart, 6:7], AF.Exp, scale=-0.5))
            sr = C["s"]

            @block.vector
            def _(v, npart=npart, sr=sr):
                v.wait_ge(s_s, sr)
                v.wait_ge(s_t, C["t"])
                inc("v", v.tensor_scalar(f32a[0:npart, :], srun[0:npart, :], lncol[0:npart, 3:4], None, ALU.add))
                inc("v", v.tensor_scalar(BC[0:npart, 0:4096], f32a[0:npart, :], lncol[0:npart, 6:7], None, ALU.mult))
            vxn = C["v"]

            for kg in range(8):
                @block.tensor
                def _(t, kg=kg, npart=npart, vxn=vxn):
                    t.wait_ge(s_v, vxn)
                    if kg == 0:
                        t.wait_ge(s_l, const_l)
                    t.wait_ge(s_s, C["s"])
                    for q in range(4):
                        k = kg * 4 + q
                        inc("t", t.transpose(pbT[0:128, q * 128:q * 128 + npart],
                                             BC[0:npart, k * 128:(k + 1) * 128], identb_s[0:npart, 0:npart]))
                tms = C["t"]

                @block.scalar
                def _(s, kg=kg, npart=npart, j=j, tms=tms):
                    s.wait_ge(s_t, tms)
                    for q in range(4):
                        k = kg * 4 + q
                        off = k * 1027 + (j * 128 if j < 8 else 1024)
                        inc("s", s.copy(arena[:, off:off + npart], pbT[0:128, q * 128:q * 128 + npart]))

        p0_done_s = C["s"]
        p0_done_v = C["v"]

        order = [DTT] + list(range(HBC0, HBC1)) + list(range(0, 32))
        slab_consumed = {}
        evac_ms = {0: 0, 1: 0, (0, "s"): 0, (1, "s"): 0}

        @block.gpsimd
        def _(g):
            inc("w", g.dma_start(out=wsb[:, 0:4096].rearrange("d (a q) -> d a q", q=128),
                                 in_=Win_b[order[0]:order[0] + 1].rearrange("b a d q -> b d a q")), 16)
            inc("w", g.dma_start(out=wsb[:, 4096:8192].rearrange("d (a q) -> d a q", q=128),
                                 in_=Win_b[order[1]:order[1] + 1].rearrange("b a d q -> b d a q")), 16)

        for i, p in enumerate(order):
            par = i % 2
            slot = par * 4096
            is_hbc = HBC0 <= p < HBC1
            pm0 = pmA if par == 0 else pmC
            pm1 = pmB if par == 0 else pmD

            @block.tensor
            def _(t, i=i, p=p, slot=slot, pm0=pm0, pm1=pm1, is_hbc=is_hbc, par=par):
                t.wait_ge(s_w, 16 * (i + 1))
                if i == 0:
                    t.wait_ge(s_s, p0_done_s)
                    t.wait_ge(s_v, p0_done_v)
                else:
                    t.wait_ge(s_v, evac_ms[par])
                    t.wait_ge(s_s, evac_ms[(par, "s")])
                for k in range(32):
                    lhs = wsb[:, slot + k * 128: slot + (k + 1) * 128]
                    inc("t", t.matmul(pm0[:, 0:512], lhs, arena[:, k * 1027:k * 1027 + 512],
                                      start=(k == 0), stop=(k == 31)))
                for k in range(32):
                    lhs = wsb[:, slot + k * 128: slot + (k + 1) * 128]
                    inc("t", t.matmul(pm1[:, 0:512], lhs, arena[:, k * 1027 + 512:k * 1027 + 1024],
                                      start=(k == 0), stop=(k == 31)))
                if is_hbc:
                    for k in range(32):
                        lhs = wsb[:, slot + k * 128: slot + (k + 1) * 128]
                        inc("t", t.matmul(pmH[:, par * 4:par * 4 + 3], lhs,
                                          arena[:, k * 1027 + 1024:k * 1027 + 1027],
                                          start=(k == 0), stop=(k == 31)))
            slab_consumed[i] = C["t"]
            mm_ms = C["t"]

            if i + 2 < len(order):
                @block.gpsimd
                def _(g, i=i, par=par):
                    g.wait_ge(s_t, slab_consumed[i])
                    inc("w", g.dma_start(out=wsb[:, par * 4096:par * 4096 + 4096].rearrange("d (a q) -> d a q", q=128),
                                         in_=Win_b[order[i + 2]:order[i + 2] + 1].rearrange("b a d q -> b d a q")), 16)

            if p == DTT:
                @block.vector
                def _(v, pm0=pm0, pm1=pm1, mm_ms=mm_ms):
                    v.wait_ge(s_t, mm_ms)
                    inc("v", v.tensor_copy(dt64[:, 0:512], pm0[0:64, 0:512]))
                    inc("v", v.tensor_copy(dt64[:, 512:1024], pm1[0:64, 0:512]))
                dtv = C["v"]
                evac_ms[par] = C["v"]
                evac_ms[(par, "s")] = C["s"]

                @block.scalar
                def _(s, dtv=dtv):
                    s.wait_ge(s_v, dtv)
                    s.wait_ge(s_l, const_l)
                    inc("s", s.activation(dtsb[:, 0:1024], dt64[:, 0:1024], AF.Abs, bias=dtb_col[:, 0:1]))
                    inc("s", s.activation(dtsb[:, 0:1024], dtsb[:, 0:1024], AF.Exp, scale=-1.0))
                    inc("s", s.activation(dtsb[:, 0:1024], dtsb[:, 0:1024], AF.Ln, bias=1.0))
                    inc("s", s.activation(Acol[:, 0:1], Alog_col[:, 0:1], AF.Exp))
                sps = C["s"]

                @block.vector
                def _(v, sps=sps):
                    v.wait_ge(s_s, sps)
                    inc("v", v.tensor_scalar(dtp[:, 0:1024], dt64[:, 0:1024], dtb_col[:, 0:1], 0.0, ALU.add, ALU.max))
                    inc("v", v.tensor_tensor(dtp[:, 0:1024], dtp[:, 0:1024], dtsb[:, 0:1024], ALU.add))
                    inc("v", v.tensor_scalar(dtsb[:, 0:1024], dtp[:, 0:1024], Acol[:, 0:1], -1.0, ALU.mult, ALU.mult))
                    for c in range(NQ):
                        inc("v", v.tensor_tensor_scan(cum[:, c * Q:(c + 1) * Q], ones64[:, 0:Q],
                                                      dtsb[:, c * Q:(c + 1) * Q], 0.0, ALU.mult, ALU.add))
                cumv0 = C["v"]

                @block.scalar
                def _(s, cumv0=cumv0):
                    s.wait_ge(s_v, cumv0)
                    for c in range(NQ):
                        inc("s", s.copy(ends[:, c:c + 1], cum[:, c * Q + Q - 1:c * Q + Q]))
                endss = C["s"]

                @block.vector
                def _(v, endss=endss):
                    v.wait_ge(s_s, endss)
                    inc("v", v.tensor_tensor_scan(scanned[:, 0:NQ], ones64[:, 0:NQ], ends[:, 0:NQ],
                                                  0.0, ALU.mult, ALU.add))
                    inc("v", v.memset(pref[:, 0:1], 0.0))
                scv = C["v"]

                @block.scalar
                def _(s, scv=scv):
                    s.wait_ge(s_v, scv)
                    inc("s", s.copy(pref[:, 1:NQ], scanned[:, 0:NQ - 1]))
                cumv = C["v"]

                for c in range(NQ):
                    @block.tensor
                    def _(t, c=c, cumv=cumv):
                        t.wait_ge(s_v, cumv)
                        t.wait_ge(s_s, C["s"])
                        inc("t", t.matmul(pmS[:, 0:64], cum[:, c * Q:(c + 1) * Q], identf_s[0:64, 0:64],
                                          start=True, stop=True))
                        inc("t", t.matmul(pmS[:, 64:128], dtp[:, c * Q:(c + 1) * Q], identf_s[0:64, 0:64],
                                          start=True, stop=True))
                    ctm = C["t"]

                    @block.scalar
                    def _(s, c=c, ctm=ctm):
                        s.wait_ge(s_t, ctm)
                        inc("s", s.copy(cumT[:, c * 64:(c + 1) * 64], pmS[:, 0:64]))
                        inc("s", s.copy(dtpT[:, c * 64:(c + 1) * 64], pmS[:, 64:128]))

                @block.sync
                def _(sy, cumv=cumv):
                    sy.wait_ge(s_v, cumv)
                    sy.wait_ge(s_s, C["s"])
                    inc("l", sy.dma_start(out=stage[0:1, 0:512], in_=ends[:, 0:NQ]), 16)
                    inc("l", sy.dma_start(out=stage[0:1, 512:1024], in_=pref[:, 0:NQ]), 16)
                stg = C["l"]

                @block.tensor
                def _(t, stg=stg):
                    t.wait_ge(s_l, stg)
                    t.wait_ge(s_s, C["s"])
                    inc("t", t.matmul(pmS[:, 0:512], onesf[0:1, :], stage[0:1, 0:512], start=True, stop=True))
                rep1 = C["t"]

                @block.vector
                def _(v, rep1=rep1):
                    v.wait_ge(s_t, rep1)
                    inc("v", v.tensor_copy(ends_rep[:, 0:512], pmS[:, 0:512]))
                rep1v = C["v"]

                @block.scalar
                def _(s, rep1v=rep1v):
                    s.wait_ge(s_v, rep1v)
                    inc("s", s.activation(dq_rep[:, 0:512], ends_rep[:, 0:512], AF.Exp))

                @block.tensor
                def _(t, rep1v=rep1v):
                    t.wait_ge(s_v, rep1v)
                    inc("t", t.matmul(pmS[:, 0:512], onesf[0:1, :], stage[0:1, 512:1024], start=True, stop=True))
                rep2 = C["t"]

                @block.vector
                def _(v, rep2=rep2):
                    v.wait_ge(s_t, rep2)
                    inc("v", v.tensor_copy(pref_rep[:, 0:512], pmS[:, 0:512]))
                rep2v = C["v"]

                @block.tensor
                def _(t, rep2v=rep2v):
                    t.wait_ge(s_v, rep2v)
                    inc("t", t.matmul(pmS[:, 0:64], onesf[0:1, :], Drow[0:1, 0:64], start=True, stop=True))
                repD = C["t"]

                @block.vector
                def _(v, repD=repD):
                    v.wait_ge(s_t, repD)
                    inc("v", v.tensor_copy(D_rep[:, 0:64], pmS[:, 0:64]))
                evac_ms[par] = C["v"]
                evac_ms[(par, "s")] = C["s"]
            elif is_hbc:
                jj = p - HBC0

                @block.vector
                def _(v, pm0=pm0, pm1=pm1, mm_ms=mm_ms, jj=jj, par=par):
                    v.wait_ge(s_t, mm_ms)
                    v.wait_ge(s_s, C["s"])
                    inc("v", v.tensor_copy(f32a[:, 0:3], pmH[:, par * 4:par * 4 + 3]))
                    inc("v", v.tensor_copy(f32a[:, 3:515], pm0[:, 0:512]))
                    inc("v", v.tensor_copy(f32a[:, 515:1027], pm1[:, 0:512]))
                    inc("v", v.tensor_scalar(f32a[:, 1027:2051], f32a[:, 0:1024], convw_sb[:, jj * 4:jj * 4 + 1],
                                             convb_sb[:, jj:jj + 1], ALU.mult, ALU.add))
                    inc("v", v.scalar_tensor_tensor(f32a[:, 2051:3075], f32a[:, 1:1025],
                                                    convw_sb[:, jj * 4 + 1:jj * 4 + 2], f32a[:, 1027:2051],
                                                    ALU.mult, ALU.add))
                    inc("v", v.scalar_tensor_tensor(f32a[:, 1027:2051], f32a[:, 2:1026],
                                                    convw_sb[:, jj * 4 + 2:jj * 4 + 3], f32a[:, 2051:3075],
                                                    ALU.mult, ALU.add))
                    inc("v", v.scalar_tensor_tensor(f32a[:, 2051:3075], f32a[:, 3:1027],
                                                    convw_sb[:, jj * 4 + 3:jj * 4 + 4], f32a[:, 1027:2051],
                                                    ALU.mult, ALU.add))
                evac_ms[par] = C["v"]
                cvv = C["v"]

                @block.scalar
                def _(s, jj=jj, cvv=cvv):
                    s.wait_ge(s_v, cvv)
                    if jj < 32:
                        s.wait_ge(s_p, evac_ms.get(("xsp", jj - 1), 0))
                        inc("s", s.activation(xstage[:, 0:1024], f32a[:, 2051:3075], AF.Silu))
                    else:
                        inc("s", s.activation(BC[:, (jj - 32) * 1024:(jj - 31) * 1024], f32a[:, 2051:3075], AF.Silu))
                evac_ms[(par, "s")] = C["s"]
                sil = C["s"]
                if jj < 32:
                    @block.sync
                    def _(sy, jj=jj, sil=sil):
                        sy.wait_ge(s_s, sil)
                        evac_ms[("xsp", jj)] = inc("p", sy.dma_start(out=d_xs[:, jj * 1024:(jj + 1) * 1024],
                                                                     in_=xstage[:, 0:1024]), 16)
            else:
                @block.scalar
                def _(s, pm0=pm0, pm1=pm1, mm_ms=mm_ms, p=p):
                    s.wait_ge(s_t, mm_ms)
                    s.wait_ge(s_v, evac_ms.get(("gsv", p - 1), 0))
                    inc("s", s.activation(silu_t[:, 0:512], pm0[:, 0:512], AF.Silu))
                    inc("s", s.activation(silu_t[:, 512:1024], pm1[:, 0:512], AF.Silu))
                evac_ms[(par, "s")] = C["s"]
                gss = C["s"]

                @block.vector
                def _(v, p=p, gss=gss):
                    v.wait_ge(s_s, gss)
                    v.wait_ge(s_p, evac_ms.get(("gsp", p - 1), 0))
                    inc("v", v.tensor_scalar(gstage[:, 0:1024], silu_t[:, 0:1024], nw_sb[:, p:p + 1], None, ALU.mult))
                evac_ms[par] = C["v"]
                evac_ms[("gsv", p)] = C["v"]
                gsv = C["v"]

                @block.sync
                def _(sy, p=p, gsv=gsv):
                    sy.wait_ge(s_v, gsv)
                    evac_ms[("gsp", p)] = inc("p", sy.dma_start(out=d_gate[:, p * 1024:(p + 1) * 1024],
                                                                in_=gstage[:, 0:1024]), 16)

        p1_t, p1_v, p1_s, p1_p = C["t"], C["v"], C["s"], C["p"]

        seq = []
        for j in range(32):
            seq += [j, j + 32]

        @block.gpsimd
        def _(g):
            g.wait_ge(s_t, p1_t)
            inc("w", g.dma_start(out=wsb[:, 0:4096].rearrange("d (a q) -> d a q", q=128),
                                 in_=W12_b[seq[0]:seq[0] + 1].rearrange("b a d q -> b d a q")), 16)
            inc("w", g.dma_start(out=wsb[:, 4096:8192].rearrange("d (a q) -> d a q", q=128),
                                 in_=W12_b[seq[1]:seq[1] + 1].rearrange("b a d q -> b d a q")), 16)
        p1_w_units = C["w"] // 16

        ff_evac = {0: p1_v, (0, "s"): p1_s}
        for jp in range(32):
            @block.tensor
            def _(t, jp=jp):
                t.wait_ge(s_w, 16 * (p1_w_units - 2 + 2 * jp + 1))
                t.wait_ge(s_v, ff_evac[0])
                t.wait_ge(s_s, ff_evac[(0, "s")])
                for k in range(32):
                    lhs = wsb[:, k * 128:(k + 1) * 128]
                    inc("t", t.matmul(pmA[:, 0:512], lhs, arena[:, k * 1027:k * 1027 + 512],
                                      start=(k == 0), stop=(k == 31)))
                for k in range(32):
                    lhs = wsb[:, k * 128:(k + 1) * 128]
                    inc("t", t.matmul(pmB[:, 0:512], lhs, arena[:, k * 1027 + 512:k * 1027 + 1024],
                                      start=(k == 0), stop=(k == 31)))
            x1ms = C["t"]

            if jp + 1 < 32:
                @block.gpsimd
                def _(g, jp=jp, x1ms=x1ms):
                    g.wait_ge(s_t, x1ms)
                    inc("w", g.dma_start(out=wsb[:, 0:4096].rearrange("d (a q) -> d a q", q=128),
                                         in_=W12_b[seq[2 * jp + 2]:seq[2 * jp + 2] + 1].rearrange("b a d q -> b d a q")), 16)

            @block.tensor
            def _(t, jp=jp):
                t.wait_ge(s_w, 16 * (p1_w_units - 2 + 2 * jp + 2))
                for k in range(32):
                    lhs = wsb[:, 4096 + k * 128:4096 + (k + 1) * 128]
                    inc("t", t.matmul(pmC[:, 0:512], lhs, arena[:, k * 1027:k * 1027 + 512],
                                      start=(k == 0), stop=(k == 31)))
                for k in range(32):
                    lhs = wsb[:, 4096 + k * 128:4096 + (k + 1) * 128]
                    inc("t", t.matmul(pmD[:, 0:512], lhs, arena[:, k * 1027 + 512:k * 1027 + 1024],
                                      start=(k == 0), stop=(k == 31)))
            x2ms = C["t"]

            if jp + 1 < 32:
                @block.gpsimd
                def _(g, jp=jp, x2ms=x2ms):
                    g.wait_ge(s_t, x2ms)
                    inc("w", g.dma_start(out=wsb[:, 4096:8192].rearrange("d (a q) -> d a q", q=128),
                                         in_=W12_b[seq[2 * jp + 3]:seq[2 * jp + 3] + 1].rearrange("b a d q -> b d a q")), 16)

            @block.scalar
            def _(s, jp=jp, x1ms=x1ms):
                s.wait_ge(s_t, x1ms)
                s.wait_ge(s_v, ff_evac[0])
                inc("s", s.activation(silu_t[:, 0:512], pmA[:, 0:512], AF.Silu, bias=b12_sb[:, jp:jp + 1]))
                inc("s", s.activation(silu_t[:, 512:1024], pmB[:, 0:512], AF.Silu, bias=b12_sb[:, jp:jp + 1]))
            sls = C["s"]

            @block.vector
            def _(v, jp=jp, x2ms=x2ms, sls=sls):
                v.wait_ge(s_t, x2ms)
                v.wait_ge(s_s, sls)
                v.wait_ge(s_p, ff_evac.get(("x4sp", jp - 1), 0))
                inc("v", v.scalar_tensor_tensor(x4stage[:, 0:512], pmC[:, 0:512], b12_sb[:, 32 + jp:33 + jp],
                                                silu_t[:, 0:512], ALU.add, ALU.mult))
                inc("v", v.scalar_tensor_tensor(x4stage[:, 512:1024], pmD[:, 0:512], b12_sb[:, 32 + jp:33 + jp],
                                                silu_t[:, 512:1024], ALU.add, ALU.mult))
            ff_evac[0] = C["v"]
            ff_evac[(0, "s")] = C["s"]
            x4v = C["v"]

            @block.sync
            def _(sy, jp=jp, x4v=x4v):
                sy.wait_ge(s_v, x4v)
                ff_evac[("x4sp", jp)] = inc("p", sy.dma_start(out=d_x4[:, jp * 1024:(jp + 1) * 1024],
                                                              in_=x4stage[:, 0:1024]), 16)

        p1b_t, p1b_v, p1b_s = C["t"], C["v"], C["s"]

        @block.vector
        def _(v):
            v.wait_ge(s_t, p1b_t)
            inc("v", v.memset(srun[:, :], 0.0))
            inc("v", v.memset(sbf[:, :], 0.0))

        for c in range(NQ):
            @block.sync
            def _(sy, c=c):
                sy.wait_ge(s_t, C["t"])
                sy.wait_ge(s_p, p1_p)
                inc("l", sy.dma_start(out=wsb[:, 0:4096].rearrange("p (k t) -> p k t", t=128),
                                      in_=d_xs.rearrange("p (k u) -> p k u", u=1024)[:, :, c * 128:(c + 1) * 128]), 16)
            xchl = C["l"]

            for h in range(H):
                g = h // 8
                par = h % 2
                if h % 8 == 0:
                    @block.tensor
                    def _(t, g=g, c=c, xchl=xchl):
                        t.wait_ge(s_v, C["v"])
                        t.wait_ge(s_s, C["s"])
                        if h == 0 and g == 0:
                            t.wait_ge(s_l, xchl)
                        inc("t", t.matmul(pmS[:, 0:128], BC[:, g * 1024 + c * 128:g * 1024 + (c + 1) * 128],
                                          BC[:, (8 + g) * 1024 + c * 128:(8 + g) * 1024 + (c + 1) * 128],
                                          start=True, stop=True))
                        inc("t", t.transpose(pbT[:, 0:128], BC[:, g * 1024 + c * 128:g * 1024 + (c + 1) * 128],
                                             identb_s[:, :]))
                    gms = C["t"]

                    @block.scalar
                    def _(s, gms=gms):
                        s.wait_ge(s_t, gms)
                        inc("s", s.copy(G_bf[:, :], pmS[:, 0:128]))
                        inc("s", s.copy(BT_sb[:, :], pbT[:, 0:128]))

                if h % 4 == 0:
                    spar = (h // 4) % 2

                    @block.sync
                    def _(sy, h=h, c=c, spar=spar):
                        sy.wait_ge(s_t, C["t"])
                        inc("l", sy.dma_start(
                            out=stage[0:1, spar * 512:spar * 512 + 512],
                            in_=cum[h:h + 4, c * Q:(c + 1) * Q]), 16)
                    stl = C["l"]

                    @block.tensor
                    def _(t, spar=spar, stl=stl, xchl=xchl):
                        t.wait_ge(s_l, max(stl, xchl))
                        t.wait_ge(s_v, C["v"])
                        t.wait_ge(s_s, C["s"])
                        inc("t", t.matmul(pmA[:, 0:512], onesf[0:1, :], stage[0:1, spar * 512:spar * 512 + 512],
                                          start=True, stop=True))
                    ems = C["t"]

                    @block.vector
                    def _(v, h=h, c=c, ems=ems):
                        v.wait_ge(s_t, ems)
                        for q in range(4):
                            inc("v", v.scalar_tensor_tensor(e_sb[:, q * 128:(q + 1) * 128], pmA[:, q * 128:(q + 1) * 128],
                                                            cumT[:, c * 64 + h + q:c * 64 + h + q + 1], AM[:, :],
                                                            ALU.subtract, ALU.add))
                    ev = C["v"]

                    @block.scalar
                    def _(s, ev=ev):
                        s.wait_ge(s_v, ev)
                        inc("s", s.activation(Dm[:, 0:512], e_sb[:, 0:512], AF.Exp))

                @block.vector
                def _(v, h=h):
                    v.wait_ge(s_s, C["s"])
                    inc("v", v.tensor_tensor(Mp[:, (h % 2) * 128:(h % 2) * 128 + 128],
                                             Dm[:, (h % 4) * 128:(h % 4) * 128 + 128], G_bf[:, :], ALU.mult))
                mpv = C["v"]

                @block.tensor
                def _(t, h=h, par=par):
                    t.wait_ge(s_s, C["s"])
                    bb = (h % 2) * 64
                    inc("t", t.transpose(pbT[:, 128 + par * 64:128 + par * 64 + 64],
                                         wsb[bb:bb + 64, (h // 2) * 128:(h // 2) * 128 + 128],
                                         identb_s[bb:bb + 64, bb:bb + 64]))
                xtm = C["t"]

                @block.scalar
                def _(s, h=h, par=par, c=c, xtm=xtm):
                    s.wait_ge(s_t, xtm)
                    s.wait_ge(s_v, C["v"])
                    inc("s", s.copy(xts[:, par * 64:par * 64 + 64], pbT[:, 128 + par * 64:128 + par * 64 + 64]))
                    inc("s", s.activation(colb[:, par * 2:par * 2 + 1], cumT[:, c * 64 + h:c * 64 + h + 1], AF.Exp))
                    inc("s", s.activation(colb[:, par * 2 + 1:par * 2 + 2], cumT[:, c * 64 + h:c * 64 + h + 1],
                                          AF.Exp, scale=-1.0, bias=ends_rep[:, h * 8 + c:h * 8 + c + 1]))
                xtc = C["s"]

                @block.vector
                def _(v, h=h, par=par, c=c, xtc=xtc):
                    v.wait_ge(s_s, xtc)
                    inc("v", v.tensor_scalar(dtx[:, par * 64:par * 64 + 64], xts[:, par * 64:par * 64 + 64],
                                             dtpT[:, c * 64 + h:c * 64 + h + 1], None, ALU.mult))
                    inc("v", v.tensor_scalar(dtxs[:, par * 64:par * 64 + 64], dtx[:, par * 64:par * 64 + 64],
                                             colb[:, par * 2 + 1:par * 2 + 2], None, ALU.mult))
                dxv = C["v"]

                @block.tensor
                def _(t, h=h, g=g, c=c, par=par, dxv=dxv):
                    t.wait_ge(s_v, dxv)
                    inc("t", t.matmul(pmC[:, par * 64:par * 64 + 64], Mp[:, (h % 2) * 128:(h % 2) * 128 + 128],
                                      dtx[:, par * 64:par * 64 + 64], start=True, stop=True))
                    inc("t", t.matmul(pmC[:, 128 + par * 64:128 + par * 64 + 64],
                                      BC[:, (8 + g) * 1024 + c * 128:(8 + g) * 1024 + (c + 1) * 128],
                                      sbf[:, h * 64:(h + 1) * 64], start=True, stop=True))
                    inc("t", t.matmul(pmC[:, 256 + par * 64:256 + par * 64 + 64], BT_sb[:, :],
                                      dtxs[:, par * 64:par * 64 + 64], start=True, stop=True))
                yms = C["t"]

                @block.vector
                def _(v, h=h, c=c, par=par, yms=yms):
                    v.wait_ge(s_t, yms)
                    inc("v", v.tensor_scalar(yg1[:, par * 64:par * 64 + 64], pmC[:, 128 + par * 64:128 + par * 64 + 64],
                                             colb[:, par * 2:par * 2 + 1], None, ALU.mult))
                    inc("v", v.scalar_tensor_tensor(yg2[:, par * 64:par * 64 + 64], xts[:, par * 64:par * 64 + 64],
                                                    D_rep[:, h:h + 1], pmC[:, par * 64:par * 64 + 64],
                                                    ALU.mult, ALU.add))
                    inc("v", v.tensor_tensor(arena[:, c * 4096 + h * 64:c * 4096 + (h + 1) * 64],
                                             yg1[:, par * 64:par * 64 + 64], yg2[:, par * 64:par * 64 + 64], ALU.add))
                    inc("v", v.tensor_scalar(yg1[:, par * 64:par * 64 + 64], srun[:, h * 64:(h + 1) * 64],
                                             dq_rep[:, h * 8 + c:h * 8 + c + 1], None, ALU.mult))
                    inc("v", v.tensor_tensor(srun[:, h * 64:(h + 1) * 64], yg1[:, par * 64:par * 64 + 64],
                                             pmC[:, 256 + par * 64:256 + par * 64 + 64], ALU.add))

            @block.scalar
            def _(s, c=c):
                s.wait_ge(s_v, C["v"])
                inc("s", s.copy(sbf[:, :], srun[:, :]))

        p2_t, p2_v, p2_s = C["t"], C["v"], C["s"]

        @block.sync
        def _(sy):
            sy.wait_ge(s_s, p2_s)
            sy.wait_ge(s_v, p2_v)
            inc("p", sy.dma_start(out=cc_in.ap()[:, 0:4096], in_=srun[:, :]), 16)
            inc("p", sy.dma_start(out=cc_in.ap()[0:64, 4096:4097], in_=scanned[:, NQ - 1:NQ],
                                  allow_slow_non_contiguous=True), 16)
        blobp = C["p"]

        @block.gpsimd
        def _(g):
            g.wait_ge(s_p, blobp)
            inc("cc", g.collective_compute("AllGather", ALU.bypass, replica_groups=[list(range(NCORE))],
                                           ins=[cc_in.ap()], outs=[cc_out.ap()]))

        @block.sync
        def _(sy):
            sy.wait_ge(s_cc, C["cc"])
            for cp in range(NCORE):
                inc("l", sy.dma_start(out=CEall[:, cp:cp + 1], in_=cc_out.ap()[cp * 128:cp * 128 + 64, 4096:4097],
                                      allow_slow_non_contiguous=True), 16)
        cel = C["l"]

        @block.vector
        def _(v, cel=cel):
            v.wait_ge(s_l, cel)
            inc("v", v.memset(srun[:, :], 0.0))
        w8v = C["v"]

        @block.sync
        def _(sy, cel=cel, w8v=w8v):
            sy.wait_ge(s_l, cel)
            sy.wait_ge(s_t, C["t"])
            inc("l", sy.dma_start(out=stage[0:1, 0:512], in_=CEall[:, 0:NCORE]), 16)
        w8l = C["l"]

        @block.tensor
        def _(t, w8l=w8l):
            t.wait_ge(s_l, w8l)
            t.wait_ge(s_v, C["v"])
            t.wait_ge(s_s, C["s"])
            inc("t", t.matmul(pmS[:, 0:512], onesf[0:1, :], stage[0:1, 0:512], start=True, stop=True))
        w8m = C["t"]

        @block.vector
        def _(v, w8m=w8m):
            v.wait_ge(s_t, w8m)
            inc("v", v.tensor_tensor(W8_rep[:, 0:512], pmS[:, 0:512], M1sb[:, 0:512], ALU.mult))
        w8vb = C["v"]

        @block.scalar
        def _(s, w8vb=w8vb):
            s.wait_ge(s_v, w8vb)
            inc("s", s.activation(ends_rep[:, 0:512], W8_rep[:, 0:512], AF.Exp))
        w8s = C["s"]

        for cp in range(NCORE):
            @block.sync
            def _(sy, cp=cp, w8s=w8s):
                sy.wait_ge(s_v, C["v"])
                sy.wait_ge(s_s, w8s)
                inc("l", sy.dma_start(out=f32a[:, :], in_=cc_out.ap()[cp * 128:(cp + 1) * 128, 0:4096]), 16)
            sgl = C["l"]

            @block.vector
            def _(v, cp=cp, sgl=sgl):
                v.wait_ge(s_l, sgl)
                v.wait_ge(s_s, C["s"])
                for h in range(H):
                    inc("v", v.tensor_scalar(srun[:, h * 64:(h + 1) * 64], srun[:, h * 64:(h + 1) * 64],
                                             ends_rep[:, h * 8 + cp:h * 8 + cp + 1], None, ALU.mult))
                    inc("v", v.tensor_scalar(yg1[:, 0:64], f32a[:, h * 64:(h + 1) * 64],
                                             M1sb[:, h * 8 + cp:h * 8 + cp + 1], None, ALU.mult))
                    inc("v", v.tensor_tensor(srun[:, h * 64:(h + 1) * 64], srun[:, h * 64:(h + 1) * 64],
                                             yg1[:, 0:64], ALU.add))
        sinv = C["v"]

        @block.scalar
        def _(s, sinv=sinv):
            s.wait_ge(s_v, sinv)
            inc("s", s.copy(sbf[:, :], srun[:, :]))
        sinbf = C["s"]

        for c in range(NQ):
            for h in range(H):
                g = h // 8
                par = h % 2

                @block.scalar
                def _(s, h=h, c=c, par=par, sinbf=sinbf):
                    s.wait_ge(s_s, sinbf)
                    s.wait_ge(s_v, C["v"])
                    inc("s", s.activation(colb[:, 4 + par:5 + par], cumT[:, c * 64 + h:c * 64 + h + 1], AF.Exp,
                                          bias=pref_rep[:, h * 8 + c:h * 8 + c + 1]))
                ecs = C["s"]

                @block.tensor
                def _(t, h=h, g=g, c=c, par=par, ecs=ecs):
                    t.wait_ge(s_s, ecs)
                    t.wait_ge(s_v, C["v"])
                    inc("t", t.matmul(pmC[:, par * 64:par * 64 + 64],
                                      BC[:, (8 + g) * 1024 + c * 128:(8 + g) * 1024 + (c + 1) * 128],
                                      sbf[:, h * 64:(h + 1) * 64], start=True, stop=True))
                ycm = C["t"]

                @block.vector
                def _(v, h=h, c=c, par=par, ycm=ycm):
                    v.wait_ge(s_t, ycm)
                    inc("v", v.tensor_scalar(yg1[:, par * 64:par * 64 + 64], pmC[:, par * 64:par * 64 + 64],
                                             colb[:, 4 + par:5 + par], None, ALU.mult))
                    inc("v", v.tensor_tensor(arena[:, c * 4096 + h * 64:c * 4096 + (h + 1) * 64],
                                             arena[:, c * 4096 + h * 64:c * 4096 + (h + 1) * 64],
                                             yg1[:, par * 64:par * 64 + 64], ALU.add))

        p4_t, p4_v, p4_s = C["t"], C["v"], C["s"]

        for c in range(NQ):
            for g in range(GD):
                @block.scalar
                def _(s, c=c, g=g):
                    if c == 0 and g == 0:
                        s.wait_ge(s_v, p4_v)
                    inc("s", s.activation(f32a[:, 0:512], arena[:, c * 4096 + g * 512:c * 4096 + (g + 1) * 512],
                                          AF.Square, accum_out=ss_sb[:, c * 8 + g:c * 8 + g + 1]))
        sqs = C["s"]

        @block.vector
        def _(v, sqs=sqs):
            v.wait_ge(s_s, sqs)
            inc("v", v.tensor_scalar(rg_sb[:, 0:64], ss_sb[:, 0:64], 1.0 / 512.0, 1e-5, ALU.mult, ALU.add))
        rgv = C["v"]

        @block.scalar
        def _(s, rgv=rgv):
            s.wait_ge(s_v, rgv)
            inc("s", s.activation(rg_sb[:, 0:64], rg_sb[:, 0:64], AF.Ln))
            inc("s", s.activation(rg_sb[:, 0:64], rg_sb[:, 0:64], AF.Exp, scale=-0.5))
        rgs = C["s"]

        @block.vector
        def _(v, rgs=rgs):
            v.wait_ge(s_s, rgs)
            for c in range(NQ):
                for g in range(GD):
                    inc("v", v.tensor_scalar(arena[:, c * 4096 + g * 512:c * 4096 + (g + 1) * 512],
                                             arena[:, c * 4096 + g * 512:c * 4096 + (g + 1) * 512],
                                             rg_sb[:, c * 8 + g:c * 8 + g + 1], None, ALU.mult))
        rnv = C["v"]

        last_yn = {0: 0, 1: 0}
        for khalf in range(2):
            @block.sync
            def _(sy, khalf=khalf, rnv=rnv):
                sy.wait_ge(s_v, rnv)
                sy.wait_ge(s_t, C["t"])
                sy.wait_ge(s_p, p1_p)
                inc("l", sy.dma_start(out=BC[:, 0:16384], in_=d_gate[:, khalf * 16384:(khalf + 1) * 16384]), 16)
            gl = C["l"]

            for k in range(khalf * 16, khalf * 16 + 16):
                for c in range(NQ):
                    par = c % 2

                    @block.tensor
                    def _(t, k=k, c=c, par=par, gl=gl):
                        t.wait_ge(s_l, gl)
                        t.wait_ge(s_v, C["v"])
                        inc("t", t.transpose(pbT[:, par * 128:par * 128 + 128],
                                             arena[:, c * 4096 + k * 128:c * 4096 + (k + 1) * 128],
                                             identb_s[:, :]))
                    ytm = C["t"]

                    @block.vector
                    def _(v, k=k, c=c, par=par, khalf=khalf, ytm=ytm):
                        v.wait_ge(s_t, ytm)
                        v.wait_ge(s_p, last_yn[par])
                        inc("v", v.tensor_tensor(ynstage[:, par * 128:par * 128 + 128],
                                                 pbT[:, par * 128:par * 128 + 128],
                                                 BC[:, (k - khalf * 16) * 1024 + c * 128:(k - khalf * 16) * 1024 + (c + 1) * 128],
                                                 ALU.mult))
                    ynv = C["v"]

                    @block.sync
                    def _(sy, k=k, c=c, par=par, ynv=ynv):
                        sy.wait_ge(s_v, ynv)
                        last_yn[par] = inc("p", sy.dma_start(out=d_yn[:, k * 1024 + c * 128:k * 1024 + (c + 1) * 128],
                                                             in_=ynstage[:, par * 128:par * 128 + 128]), 16)

        for tk in range(2):
            @block.sync
            def _(sy, tk=tk):
                sy.wait_ge(s_p, C["p"])
                sy.wait_ge(s_t, C["t"])
                sy.wait_ge(s_s, C["s"])
                inc("l", sy.dma_start(out=arena[:, 0:16384].rearrange("p (k t) -> p k t", t=512),
                                      in_=d_yn.rearrange("p (k u) -> p k u", u=1024)[:, :, tk * 512:tk * 512 + 512]), 16)
                inc("l", sy.dma_start(out=arena[:, 16384:32768].rearrange("p (k t) -> p k t", t=512),
                                      in_=d_x4.rearrange("p (k u) -> p k u", u=1024)[:, :, tk * 512:tk * 512 + 512]), 16)
            rhl = C["l"]

            @block.gpsimd
            def _(g, tk=tk):
                g.wait_ge(s_t, C["t"])
                inc("w", g.dma_start(out=wsb[:, 0:4096].rearrange("d (a q) -> d a q", q=128),
                                     in_=Wout_b[0:1].rearrange("b a d q -> b d a q")), 16)
                inc("w", g.dma_start(out=wsb[:, 4096:8192].rearrange("d (a q) -> d a q", q=128),
                                     in_=W3_b[0:1].rearrange("b a d q -> b d a q")), 16)

            o_evac = {0: C["v"], 1: C["v"]}
            for do in range(32):
                par = do % 2
                pmo = pmA if par == 0 else pmB

                @block.tensor
                def _(t, do=do, pmo=pmo, par=par, rhl=rhl):
                    t.wait_ge(s_l, rhl)
                    t.wait_ge(s_w, C["w"])
                    t.wait_ge(s_v, o_evac[par])
                    for k in range(32):
                        inc("t", t.matmul(pmo[:, 0:512], wsb[:, k * 128:(k + 1) * 128],
                                          arena[:, k * 512:(k + 1) * 512], start=(k == 0), stop=False))
                    for k in range(32):
                        inc("t", t.matmul(pmo[:, 0:512], wsb[:, 4096 + k * 128:4096 + (k + 1) * 128],
                                          arena[:, 16384 + k * 512:16384 + (k + 1) * 512],
                                          start=False, stop=(k == 31)))
                oms = C["t"]

                if do + 1 < 32:
                    @block.gpsimd
                    def _(g, do=do, oms=oms):
                        g.wait_ge(s_t, oms)
                        inc("w", g.dma_start(out=wsb[:, 0:4096].rearrange("d (a q) -> d a q", q=128),
                                             in_=Wout_b[do + 1:do + 2].rearrange("b a d q -> b d a q")), 16)
                        inc("w", g.dma_start(out=wsb[:, 4096:8192].rearrange("d (a q) -> d a q", q=128),
                                             in_=W3_b[do + 1:do + 2].rearrange("b a d q -> b d a q")), 16)

                @block.sync
                def _(sy, do=do, tk=tk, par=par):
                    sy.wait_ge(s_v, o_evac.get((par, "xt"), 0))
                    inc("l", sy.dma_start(out=f32a[:, par * 512:par * 512 + 512],
                                          in_=xT_loc[do * 128:(do + 1) * 128, tk * 512:tk * 512 + 512]), 16)
                xtl = C["l"]

                @block.vector
                def _(v, do=do, pmo=pmo, par=par, oms=oms, xtl=xtl):
                    v.wait_ge(s_t, oms)
                    v.wait_ge(s_l, xtl)
                    v.wait_ge(s_o, o_evac.get((par, "op"), 0))
                    inc("v", v.scalar_tensor_tensor(f32a[:, 2048 + par * 512:2048 + par * 512 + 512],
                                                    pmo[:, 0:512], b3_sb[:, do:do + 1],
                                                    f32a[:, par * 512:par * 512 + 512], ALU.add, ALU.add))
                o_evac[par] = C["v"]
                o_evac[(par, "xt")] = C["v"]
                ov = C["v"]

                @block.sync
                def _(sy, do=do, tk=tk, par=par, ov=ov):
                    sy.wait_ge(s_v, ov)
                    o_evac[(par, "op")] = inc("o", sy.dma_start(
                        out=outT[do * 128:(do + 1) * 128, tk * 512:tk * 512 + 512],
                        in_=f32a[:, 2048 + par * 512:2048 + par * 512 + 512]), 16)

        @block.sync
        def _(sy):
            sy.wait_ge(s_o, C["o"])

    return nc


_NC_CACHE = [None]


def _host_prep(inp):
    f32 = np.float32
    x = np.asarray(inp["L0_attn_hidden_states"], f32).reshape(L, DM)
    Win = np.asarray(inp["L0_attn_in_proj_w"], f32)
    Wout = np.asarray(inp["L0_attn_out_proj_w"], f32)
    W12 = np.asarray(inp["L0_ffn_w12"], f32)
    W3 = np.asarray(inp["L0_ffn_w3"], f32)

    def blockT(W, npt):
        out_dim = W.shape[0]
        pad = npt * 128 - out_dim
        if pad:
            W = np.concatenate([W, np.zeros((pad, W.shape[1]), f32)], axis=0)
        WT = W.T
        B = WT.reshape(32, 128, npt, 128).transpose(2, 0, 1, 3)
        return np.ascontiguousarray(B)

    Win_b = blockT(Win, NPT)
    Wout_b = blockT(Wout, 32)
    W12_b = blockT(W12, 64)
    W3_b = blockT(W3, 32)
    xT = np.ascontiguousarray(x.T)

    identb = np.eye(128, dtype=ml_dtypes.bfloat16)
    identf = np.eye(128, dtype=f32)
    t_i = np.arange(128)
    AM = np.where(t_i[None, :] >= t_i[:, None], 0.0, -1e30).astype(f32)

    maps = []
    for c in range(NCORE):
        t0 = c * LC
        halo = x[t0 - 3:t0] if c > 0 else np.zeros((3, DM), f32)
        m1row = np.zeros((1, H * NCORE), f32)
        for hh in range(H):
            for jj in range(NCORE):
                if jj < c:
                    m1row[0, hh * NCORE + jj] = 1.0
        M1full = np.ascontiguousarray(np.repeat(m1row, 128, axis=0))
        maps.append({
            "x_loc": np.ascontiguousarray(x[t0:t0 + LC]),
            "x_halo": np.ascontiguousarray(halo),
            "xT_loc": np.ascontiguousarray(xT[:, t0:t0 + LC]),
            "Win_b": Win_b, "Wout_b": Wout_b, "W12_b": W12_b, "W3_b": W3_b,
            "conv_w": np.asarray(inp["L0_attn_conv_weight"], f32),
            "conv_b": np.asarray(inp["L0_attn_conv_bias"], f32),
            "dt_bias": np.asarray(inp["L0_attn_dt_bias"], f32),
            "A_log": np.asarray(inp["L0_attn_A_log"], f32),
            "D_in": np.asarray(inp["L0_attn_D"], f32),
            "nw_in": np.asarray(inp["L0_attn_norm_weight"], f32),
            "b12_in": np.asarray(inp["L0_ffn_b12"], f32),
            "b3_in": np.asarray(inp["L0_ffn_b3"], f32),
            "identb": identb, "identf": identf, "AM_in": AM,
            "M1_in": M1full,
        })
    return maps


def kernel(**inp):
    maps = _host_prep(inp)
    if _NC_CACHE[0] is None:
        _NC_CACHE[0] = build_nc()
    nc = _NC_CACHE[0]
    res = run_bass_kernel_spmd(nc, maps, core_ids=list(range(NCORE)))
    outs = [np.asarray(res.results[c]["outT"]) for c in range(NCORE)]
    outT = np.concatenate(outs, axis=1)
    return np.ascontiguousarray(outT.T).reshape(1, L, DM).astype(np.float32)


if __name__ == "__main__":
    import time
    t0 = time.time()
    nc = build_nc()
    print("build ok", time.time() - t0, "s")


# version 05
# speedup vs baseline: 1.0923x; 1.0923x over previous
import numpy as np
import ml_dtypes
import concourse.bass as bass
import concourse.mybir as mybir
from concourse.bass_utils import run_bass_kernel_spmd
from contextlib import ExitStack

F32 = mybir.dt.float32
BF16 = mybir.dt.bfloat16
AF = mybir.ActivationFunctionType
ALU = mybir.AluOpType
AXX = mybir.AxisListType.X

H = 64; PD = 64; ND = 128; GD = 8; KC = 4
DM = 4096; ID = 4096; GN = 1024; CONV = 6144; PROJ = 10304
L = 8192; NCORE = 8; LC = 1024; Q = 128; NQ = 8
NPT = 81
HBC0, HBC1 = 32, 80
DTT = 80


def build_nc():
    nc = bass.Bass("TRN2", target_bir_lowering=False)
    def din(name, shape, dt=F32):
        return nc.dram_tensor(name, shape, dt, kind="ExternalInput").ap()

    x_loc = din("x_loc", [LC, DM])
    x_halo = din("x_halo", [3, DM])
    xT_loc = din("xT_loc", [DM, LC])
    Win_b = din("Win_b", [NPT, 32, 128, 128])
    Wout_b = din("Wout_b", [32, 32, 128, 128])
    W12_b = din("W12_b", [64, 32, 128, 128])
    W3_b = din("W3_b", [32, 32, 128, 128])
    conv_w = din("conv_w", [CONV, KC])
    conv_b = din("conv_b", [CONV])
    dt_bias = din("dt_bias", [H])
    A_log = din("A_log", [H])
    D_in = din("D_in", [H])
    nw_in = din("nw_in", [ID])
    b12_in = din("b12_in", [2 * DM])
    b3_in = din("b3_in", [DM])
    identb = din("identb", [128, 128], BF16)
    identf = din("identf", [128, 128])
    AM_in = din("AM_in", [128, 128])
    M1_in = din("M1_in", [128, H * NCORE])

    outT = nc.dram_tensor("outT", [DM, LC], F32, kind="ExternalOutput").ap()

    d_gate = nc.dram_tensor("d_gate", [128, 32 * 1024], BF16).ap()
    d_xs = nc.dram_tensor("d_xs", [128, 32 * 1024], BF16).ap()
    d_x4 = nc.dram_tensor("d_x4", [128, 32 * 1024], BF16).ap()
    d_yn = nc.dram_tensor("d_yn", [128, 32 * 1024], BF16).ap()
    cc_in = nc.dram_tensor("cc_in", [128, 4097], F32)
    cc_out = nc.dram_tensor("cc_out", [128 * NCORE, 4097], F32)

    C = {"t": 0, "v": 0, "s": 0, "w": 0, "l": 0, "p": 0, "o": 0, "cc": 0}

    with ExitStack() as ctx:
        E = ctx.enter_context
        arena = E(nc.sbuf_tensor("arena", [128, 32 * 1027], BF16))
        BC = E(nc.sbuf_tensor("BC", [128, 16384], BF16))
        wsb = E(nc.sbuf_tensor("wsb", [128, 8192], BF16))
        f32a = E(nc.sbuf_tensor("f32a", [128, 4096], F32))
        srun = E(nc.sbuf_tensor("srun", [128, 4096], F32))
        sbf = E(nc.sbuf_tensor("sbf", [128, 4096], BF16))
        cum = E(nc.sbuf_tensor("cum", [H, LC], F32))
        dtp = E(nc.sbuf_tensor("dtp", [H, LC], F32))
        dtsb = E(nc.sbuf_tensor("dtsb", [H, LC], F32))
        dt64 = E(nc.sbuf_tensor("dt64", [H, LC], F32))
        cumT = E(nc.sbuf_tensor("cumT", [128, 512], F32))
        dtpT = E(nc.sbuf_tensor("dtpT", [128, 512], F32))
        ends_rep = E(nc.sbuf_tensor("ends_rep", [128, 512], F32))
        dq_rep = E(nc.sbuf_tensor("dq_rep", [128, 512], F32))
        pref_rep = E(nc.sbuf_tensor("pref_rep", [128, 512], F32))
        W8_rep = E(nc.sbuf_tensor("W8_rep", [128, 512], F32))
        D_rep = E(nc.sbuf_tensor("D_rep", [128, 64], F32))
        Drow = E(nc.sbuf_tensor("Drow", [1, 64], F32))
        stage = E(nc.sbuf_tensor("stage", [1, 1024], F32))
        onesf = E(nc.sbuf_tensor("onesf", [1, 128], F32))
        ones64 = E(nc.sbuf_tensor("ones64", [H, 128], F32))
        identb_s = E(nc.sbuf_tensor("identb_s", [128, 128], BF16))
        identf_s = E(nc.sbuf_tensor("identf_s", [128, 128], F32))
        AM = E(nc.sbuf_tensor("AM", [128, 128], F32))
        M1sb = E(nc.sbuf_tensor("M1sb", [128, H * NCORE], F32))
        convw_sb = E(nc.sbuf_tensor("convw_sb", [128, 192], F32))
        convb_sb = E(nc.sbuf_tensor("convb_sb", [128, 48], F32))
        dtb_col = E(nc.sbuf_tensor("dtb_col", [H, 1], F32))
        Alog_col = E(nc.sbuf_tensor("Alog_col", [H, 1], F32))
        Acol = E(nc.sbuf_tensor("Acol", [H, 1], F32))
        nw_sb = E(nc.sbuf_tensor("nw_sb", [128, 32], F32))
        b12_sb = E(nc.sbuf_tensor("b12_sb", [128, 64], F32))
        b3_sb = E(nc.sbuf_tensor("b3_sb", [128, 32], F32))
        ends = E(nc.sbuf_tensor("ends", [H, NQ], F32))
        scanned = E(nc.sbuf_tensor("scanned", [H, NQ], F32))
        pref = E(nc.sbuf_tensor("pref", [H, NQ], F32))
        CEall = E(nc.sbuf_tensor("CEall", [H, NCORE], F32))
        Ssum = E(nc.sbuf_tensor("Ssum", [H, NCORE], F32))
        Sme = E(nc.sbuf_tensor("Sme", [H, 1], F32))
        tmp8 = E(nc.sbuf_tensor("tmp8", [H, NCORE], F32))
        lncol = E(nc.sbuf_tensor("lncol", [128, 8], F32))
        G_bf = E(nc.sbuf_tensor("G_bf", [128, 128], BF16))
        BT_sb = E(nc.sbuf_tensor("BT_sb", [128, 128], BF16))
        Dm = E(nc.sbuf_tensor("Dm", [128, 512], BF16))
        e_sb = E(nc.sbuf_tensor("e_sb", [128, 512], F32))
        Mp = E(nc.sbuf_tensor("Mp", [128, 256], BF16))
        xts = E(nc.sbuf_tensor("xts", [128, 128], BF16))
        dtx = E(nc.sbuf_tensor("dtx", [128, 128], BF16))
        dtxs = E(nc.sbuf_tensor("dtxs", [128, 128], BF16))
        colb = E(nc.sbuf_tensor("colb", [128, 8], F32))
        yg1 = E(nc.sbuf_tensor("yg1", [128, 128], F32))
        yg2 = E(nc.sbuf_tensor("yg2", [128, 128], F32))
        gstage = E(nc.sbuf_tensor("gstage", [128, 1024], BF16))
        xstage = E(nc.sbuf_tensor("xstage", [128, 1024], BF16))
        x4stage = E(nc.sbuf_tensor("x4stage", [128, 1024], BF16))
        silu_t = E(nc.sbuf_tensor("silu_t", [128, 1024], BF16))
        ynstage = E(nc.sbuf_tensor("ynstage", [128, 256], BF16))
        ss_sb = E(nc.sbuf_tensor("ss_sb", [128, 64], F32))
        rg_sb = E(nc.sbuf_tensor("rg_sb", [128, 64], F32))

        pmA = E(nc.psum_tensor("pmA", [128, 512], F32))
        pmB = E(nc.psum_tensor("pmB", [128, 512], F32))
        pmC = E(nc.psum_tensor("pmC", [128, 512], F32))
        pmD = E(nc.psum_tensor("pmD", [128, 512], F32))
        pmS = E(nc.psum_tensor("pmS", [128, 512], F32))
        pbT = E(nc.psum_tensor("pbT", [128, 512], BF16))
        pmH = E(nc.psum_tensor("pmH", [128, 8], F32))

        s_t = E(nc.semaphore("s_t")); s_v = E(nc.semaphore("s_v"))
        s_s = E(nc.semaphore("s_s")); s_w = E(nc.semaphore("s_w"))
        s_l = E(nc.semaphore("s_l")); s_p = E(nc.semaphore("s_p"))
        s_o = E(nc.semaphore("s_o")); s_cc = E(nc.semaphore("s_cc"))
        SEM = {"t": s_t, "v": s_v, "s": s_s, "w": s_w, "l": s_l, "p": s_p, "o": s_o, "cc": s_cc}
        block = E(nc.Block())

        def inc(key, instr, n=1):
            instr.then_inc(SEM[key], n)
            C[key] += n
            return C[key]

        @block.sync
        def _(sy):
            with nc.allow_non_contiguous_dma(reason="small one-time const loads"):
                inc("l", sy.dma_start(out=identb_s[:], in_=identb), 16)
                inc("l", sy.dma_start(out=identf_s[:], in_=identf), 16)
                inc("l", sy.dma_start(out=AM[:], in_=AM_in), 16)
                inc("l", sy.dma_start(out=M1sb[:], in_=M1_in), 16)
                inc("l", sy.dma_start(out=convb_sb[:], in_=conv_b.rearrange("(a j p) -> a p j", a=1, p=128)), 16)
                inc("l", sy.dma_start(out=convw_sb[:].rearrange("p (j k) -> p j k", k=KC),
                                      in_=conv_w.rearrange("(j p) k -> p j k", p=128)), 16)
                inc("l", sy.dma_start(out=dtb_col[:], in_=dt_bias.rearrange("(p b) -> p b", b=1)), 16)
                inc("l", sy.dma_start(out=Alog_col[:], in_=A_log.rearrange("(p b) -> p b", b=1)), 16)
                inc("l", sy.dma_start(out=Drow[:], in_=D_in.rearrange("(a k) -> a k", a=1)), 16)
                inc("l", sy.dma_start(out=nw_sb[:], in_=nw_in.rearrange("(a j p) -> a p j", a=1, p=128)), 16)
                inc("l", sy.dma_start(out=b12_sb[:], in_=b12_in.rearrange("(a j p) -> a p j", a=1, p=128)), 16)
                inc("l", sy.dma_start(out=b3_sb[:], in_=b3_in.rearrange("(a j p) -> a p j", a=1, p=128)), 16)
        const_l = C["l"]

        @block.vector
        def _(v):
            v.wait_ge(s_l, const_l)
            inc("v", v.memset(onesf[:], 1.0))
            inc("v", v.memset(ones64[:], 1.0))

        for j in range(9):
            npart = 128 if j < 8 else 3
            src = x_loc[j * 128:(j + 1) * 128, :] if j < 8 else x_halo
            ld_ms = [0]

            @block.sync
            def _(sy, src=src, npart=npart, ld_ms=ld_ms, j=j):
                if j > 0:
                    sy.wait_ge(s_v, C["v"])
                ld_ms[0] = inc("l", sy.dma_start(out=srun[0:npart, :], in_=src), 16)

            @block.scalar
            def _(s, npart=npart, ld_ms=ld_ms):
                s.wait_ge(s_l, ld_ms[0])
                s.wait_ge(s_v, C["v"])
                inc("s", s.activation(f32a[0:npart, :], srun[0:npart, :], AF.Identity, accum_out=lncol[0:npart, 0:1]))
                inc("s", s.activation(f32a[0:npart, :], srun[0:npart, :], AF.Square, accum_out=lncol[0:npart, 1:2]))
            ssq = C["s"]

            @block.vector
            def _(v, npart=npart, ssq=ssq):
                v.wait_ge(s_s, ssq)
                inc("v", v.tensor_scalar(lncol[0:npart, 2:3], lncol[0:npart, 0:1], 1.0 / DM, None, ALU.mult))
                inc("v", v.tensor_scalar(lncol[0:npart, 4:5], lncol[0:npart, 1:2], 1.0 / DM, 1e-5, ALU.mult, ALU.add))
            vvar = C["v"]

            @block.scalar
            def _(s, npart=npart, vvar=vvar):
                s.wait_ge(s_v, vvar)
                inc("s", s.activation(lncol[0:npart, 3:4], lncol[0:npart, 2:3], AF.Identity, scale=-1.0))
            sneg = C["s"]

            @block.vector
            def _(v, npart=npart, sneg=sneg):
                v.wait_ge(s_s, sneg)
                inc("v", v.scalar_tensor_tensor(lncol[0:npart, 5:6], lncol[0:npart, 2:3], lncol[0:npart, 3:4],
                                                lncol[0:npart, 4:5], ALU.mult, ALU.add))
            vve = C["v"]

            @block.scalar
            def _(s, npart=npart, vve=vve):
                s.wait_ge(s_v, vve)
                inc("s", s.activation(lncol[0:npart, 6:7], lncol[0:npart, 5:6], AF.Ln))
                inc("s", s.activation(lncol[0:npart, 6:7], lncol[0:npart, 6:7], AF.Exp, scale=-0.5))
            sr = C["s"]

            @block.vector
            def _(v, npart=npart, sr=sr):
                v.wait_ge(s_s, sr)
                v.wait_ge(s_t, C["t"])
                inc("v", v.tensor_scalar(f32a[0:npart, :], srun[0:npart, :], lncol[0:npart, 3:4], None, ALU.add))
                inc("v", v.tensor_scalar(BC[0:npart, 0:4096], f32a[0:npart, :], lncol[0:npart, 6:7], None, ALU.mult))
            vxn = C["v"]

            for kg in range(8):
                @block.tensor
                def _(t, kg=kg, npart=npart, vxn=vxn):
                    t.wait_ge(s_v, vxn)
                    if kg == 0:
                        t.wait_ge(s_l, const_l)
                    t.wait_ge(s_s, C["s"])
                    for q in range(4):
                        k = kg * 4 + q
                        inc("t", t.transpose(pbT[0:128, q * 128:q * 128 + npart],
                                             BC[0:npart, k * 128:(k + 1) * 128], identb_s[0:npart, 0:npart]))
                tms = C["t"]

                @block.scalar
                def _(s, kg=kg, npart=npart, j=j, tms=tms):
                    s.wait_ge(s_t, tms)
                    for q in range(4):
                        k = kg * 4 + q
                        off = k * 1027 + (j * 128 if j < 8 else 1024)
                        inc("s", s.copy(arena[:, off:off + npart], pbT[0:128, q * 128:q * 128 + npart]))

        p0_done_s = C["s"]
        p0_done_v = C["v"]

        order = [DTT] + list(range(HBC0, HBC1)) + list(range(0, 32))
        slab_consumed = {}
        evac_ms = {0: 0, 1: 0, (0, "s"): 0, (1, "s"): 0}

        @block.gpsimd
        def _(g):
            inc("w", g.dma_start(out=wsb[:, 0:4096].rearrange("d (a q) -> d a q", q=128),
                                 in_=Win_b[order[0]:order[0] + 1].rearrange("b a d q -> b d a q")), 16)
            inc("w", g.dma_start(out=wsb[:, 4096:8192].rearrange("d (a q) -> d a q", q=128),
                                 in_=Win_b[order[1]:order[1] + 1].rearrange("b a d q -> b d a q")), 16)

        for i, p in enumerate(order):
            par = i % 2
            slot = par * 4096
            is_hbc = HBC0 <= p < HBC1
            pm0 = pmA if par == 0 else pmC
            pm1 = pmB if par == 0 else pmD

            @block.tensor
            def _(t, i=i, p=p, slot=slot, pm0=pm0, pm1=pm1, is_hbc=is_hbc, par=par):
                t.wait_ge(s_w, 16 * (i + 1))
                if i == 0:
                    t.wait_ge(s_s, p0_done_s)
                    t.wait_ge(s_v, p0_done_v)
                else:
                    t.wait_ge(s_v, evac_ms[par])
                    t.wait_ge(s_s, evac_ms[(par, "s")])
                for k in range(32):
                    lhs = wsb[:, slot + k * 128: slot + (k + 1) * 128]
                    inc("t", t.matmul(pm0[:, 0:512], lhs, arena[:, k * 1027:k * 1027 + 512],
                                      start=(k == 0), stop=(k == 31)))
                for k in range(32):
                    lhs = wsb[:, slot + k * 128: slot + (k + 1) * 128]
                    inc("t", t.matmul(pm1[:, 0:512], lhs, arena[:, k * 1027 + 512:k * 1027 + 1024],
                                      start=(k == 0), stop=(k == 31)))
                if is_hbc:
                    for k in range(32):
                        lhs = wsb[:, slot + k * 128: slot + (k + 1) * 128]
                        inc("t", t.matmul(pmH[:, par * 4:par * 4 + 3], lhs,
                                          arena[:, k * 1027 + 1024:k * 1027 + 1027],
                                          start=(k == 0), stop=(k == 31)))
            slab_consumed[i] = C["t"]
            mm_ms = C["t"]

            if i + 2 < len(order):
                @block.gpsimd
                def _(g, i=i, par=par):
                    g.wait_ge(s_t, slab_consumed[i])
                    inc("w", g.dma_start(out=wsb[:, par * 4096:par * 4096 + 4096].rearrange("d (a q) -> d a q", q=128),
                                         in_=Win_b[order[i + 2]:order[i + 2] + 1].rearrange("b a d q -> b d a q")), 16)

            if p == DTT:
                @block.vector
                def _(v, pm0=pm0, pm1=pm1, mm_ms=mm_ms):
                    v.wait_ge(s_t, mm_ms)
                    inc("v", v.tensor_copy(dt64[:, 0:512], pm0[0:64, 0:512]))
                    inc("v", v.tensor_copy(dt64[:, 512:1024], pm1[0:64, 0:512]))
                dtv = C["v"]
                evac_ms[par] = C["v"]
                evac_ms[(par, "s")] = C["s"]

                @block.scalar
                def _(s, dtv=dtv):
                    s.wait_ge(s_v, dtv)
                    s.wait_ge(s_l, const_l)
                    inc("s", s.activation(dtsb[:, 0:1024], dt64[:, 0:1024], AF.Abs, bias=dtb_col[:, 0:1]))
                    inc("s", s.activation(dtsb[:, 0:1024], dtsb[:, 0:1024], AF.Exp, scale=-1.0))
                    inc("s", s.activation(dtsb[:, 0:1024], dtsb[:, 0:1024], AF.Ln, bias=1.0))
                    inc("s", s.activation(Acol[:, 0:1], Alog_col[:, 0:1], AF.Exp))
                sps = C["s"]

                @block.vector
                def _(v, sps=sps):
                    v.wait_ge(s_s, sps)
                    inc("v", v.tensor_scalar(dtp[:, 0:1024], dt64[:, 0:1024], dtb_col[:, 0:1], 0.0, ALU.add, ALU.max))
                    inc("v", v.tensor_tensor(dtp[:, 0:1024], dtp[:, 0:1024], dtsb[:, 0:1024], ALU.add))
                    inc("v", v.tensor_scalar(dtsb[:, 0:1024], dtp[:, 0:1024], Acol[:, 0:1], -1.0, ALU.mult, ALU.mult))
                    for c in range(NQ):
                        inc("v", v.tensor_tensor_scan(cum[:, c * Q:(c + 1) * Q], ones64[:, 0:Q],
                                                      dtsb[:, c * Q:(c + 1) * Q], 0.0, ALU.mult, ALU.add))
                cumv0 = C["v"]

                @block.scalar
                def _(s, cumv0=cumv0):
                    s.wait_ge(s_v, cumv0)
                    for c in range(NQ):
                        inc("s", s.copy(ends[:, c:c + 1], cum[:, c * Q + Q - 1:c * Q + Q]))
                endss = C["s"]

                @block.vector
                def _(v, endss=endss):
                    v.wait_ge(s_s, endss)
                    inc("v", v.tensor_tensor_scan(scanned[:, 0:NQ], ones64[:, 0:NQ], ends[:, 0:NQ],
                                                  0.0, ALU.mult, ALU.add))
                    inc("v", v.memset(pref[:, 0:1], 0.0))
                scv = C["v"]

                @block.scalar
                def _(s, scv=scv):
                    s.wait_ge(s_v, scv)
                    inc("s", s.copy(pref[:, 1:NQ], scanned[:, 0:NQ - 1]))
                cumv = C["v"]

                for c in range(NQ):
                    @block.tensor
                    def _(t, c=c, cumv=cumv):
                        t.wait_ge(s_v, cumv)
                        t.wait_ge(s_s, C["s"])
                        inc("t", t.matmul(pmS[:, 0:64], cum[:, c * Q:(c + 1) * Q], identf_s[0:64, 0:64],
                                          start=True, stop=True))
                        inc("t", t.matmul(pmS[:, 64:128], dtp[:, c * Q:(c + 1) * Q], identf_s[0:64, 0:64],
                                          start=True, stop=True))
                    ctm = C["t"]

                    @block.scalar
                    def _(s, c=c, ctm=ctm):
                        s.wait_ge(s_t, ctm)
                        inc("s", s.copy(cumT[:, c * 64:(c + 1) * 64], pmS[:, 0:64]))
                        inc("s", s.copy(dtpT[:, c * 64:(c + 1) * 64], pmS[:, 64:128]))

                @block.sync
                def _(sy, cumv=cumv):
                    sy.wait_ge(s_v, cumv)
                    sy.wait_ge(s_s, C["s"])
                    inc("l", sy.dma_start(out=stage[0:1, 0:512], in_=ends[:, 0:NQ]), 16)
                    inc("l", sy.dma_start(out=stage[0:1, 512:1024], in_=pref[:, 0:NQ]), 16)
                stg = C["l"]

                @block.tensor
                def _(t, stg=stg):
                    t.wait_ge(s_l, stg)
                    t.wait_ge(s_s, C["s"])
                    inc("t", t.matmul(pmS[:, 0:512], onesf[0:1, :], stage[0:1, 0:512], start=True, stop=True))
                rep1 = C["t"]

                @block.vector
                def _(v, rep1=rep1):
                    v.wait_ge(s_t, rep1)
                    inc("v", v.tensor_copy(ends_rep[:, 0:512], pmS[:, 0:512]))
                rep1v = C["v"]

                @block.scalar
                def _(s, rep1v=rep1v):
                    s.wait_ge(s_v, rep1v)
                    inc("s", s.activation(dq_rep[:, 0:512], ends_rep[:, 0:512], AF.Exp))

                @block.tensor
                def _(t, rep1v=rep1v):
                    t.wait_ge(s_v, rep1v)
                    inc("t", t.matmul(pmS[:, 0:512], onesf[0:1, :], stage[0:1, 512:1024], start=True, stop=True))
                rep2 = C["t"]

                @block.vector
                def _(v, rep2=rep2):
                    v.wait_ge(s_t, rep2)
                    inc("v", v.tensor_copy(pref_rep[:, 0:512], pmS[:, 0:512]))
                rep2v = C["v"]

                @block.tensor
                def _(t, rep2v=rep2v):
                    t.wait_ge(s_v, rep2v)
                    inc("t", t.matmul(pmS[:, 0:64], onesf[0:1, :], Drow[0:1, 0:64], start=True, stop=True))
                repD = C["t"]

                @block.vector
                def _(v, repD=repD):
                    v.wait_ge(s_t, repD)
                    inc("v", v.tensor_copy(D_rep[:, 0:64], pmS[:, 0:64]))
                evac_ms[par] = C["v"]
                evac_ms[(par, "s")] = C["s"]
            elif is_hbc:
                jj = p - HBC0

                @block.vector
                def _(v, pm0=pm0, pm1=pm1, mm_ms=mm_ms, jj=jj, par=par):
                    v.wait_ge(s_t, mm_ms)
                    v.wait_ge(s_s, C["s"])
                    inc("v", v.tensor_copy(f32a[:, 0:3], pmH[:, par * 4:par * 4 + 3]))
                    inc("v", v.tensor_copy(f32a[:, 3:515], pm0[:, 0:512]))
                    inc("v", v.tensor_copy(f32a[:, 515:1027], pm1[:, 0:512]))
                    inc("v", v.tensor_scalar(f32a[:, 1027:2051], f32a[:, 0:1024], convw_sb[:, jj * 4:jj * 4 + 1],
                                             convb_sb[:, jj:jj + 1], ALU.mult, ALU.add))
                    inc("v", v.scalar_tensor_tensor(f32a[:, 2051:3075], f32a[:, 1:1025],
                                                    convw_sb[:, jj * 4 + 1:jj * 4 + 2], f32a[:, 1027:2051],
                                                    ALU.mult, ALU.add))
                    inc("v", v.scalar_tensor_tensor(f32a[:, 1027:2051], f32a[:, 2:1026],
                                                    convw_sb[:, jj * 4 + 2:jj * 4 + 3], f32a[:, 2051:3075],
                                                    ALU.mult, ALU.add))
                    inc("v", v.scalar_tensor_tensor(f32a[:, 2051:3075], f32a[:, 3:1027],
                                                    convw_sb[:, jj * 4 + 3:jj * 4 + 4], f32a[:, 1027:2051],
                                                    ALU.mult, ALU.add))
                evac_ms[par] = C["v"]
                cvv = C["v"]

                @block.scalar
                def _(s, jj=jj, cvv=cvv):
                    s.wait_ge(s_v, cvv)
                    if jj < 32:
                        s.wait_ge(s_p, evac_ms.get(("xsp", jj - 1), 0))
                        inc("s", s.activation(xstage[:, 0:1024], f32a[:, 2051:3075], AF.Silu))
                    else:
                        inc("s", s.activation(BC[:, (jj - 32) * 1024:(jj - 31) * 1024], f32a[:, 2051:3075], AF.Silu))
                evac_ms[(par, "s")] = C["s"]
                sil = C["s"]
                if jj < 32:
                    @block.sync
                    def _(sy, jj=jj, sil=sil):
                        sy.wait_ge(s_s, sil)
                        evac_ms[("xsp", jj)] = inc("p", sy.dma_start(out=d_xs[:, jj * 1024:(jj + 1) * 1024],
                                                                     in_=xstage[:, 0:1024]), 16)
            else:
                @block.scalar
                def _(s, pm0=pm0, pm1=pm1, mm_ms=mm_ms, p=p):
                    s.wait_ge(s_t, mm_ms)
                    s.wait_ge(s_v, evac_ms.get(("gsv", p - 1), 0))
                    inc("s", s.activation(silu_t[:, 0:512], pm0[:, 0:512], AF.Silu))
                    inc("s", s.activation(silu_t[:, 512:1024], pm1[:, 0:512], AF.Silu))
                evac_ms[(par, "s")] = C["s"]
                gss = C["s"]

                @block.vector
                def _(v, p=p, gss=gss):
                    v.wait_ge(s_s, gss)
                    v.wait_ge(s_p, evac_ms.get(("gsp", p - 1), 0))
                    inc("v", v.tensor_scalar(gstage[:, 0:1024], silu_t[:, 0:1024], nw_sb[:, p:p + 1], None, ALU.mult))
                evac_ms[par] = C["v"]
                evac_ms[("gsv", p)] = C["v"]
                gsv = C["v"]

                @block.sync
                def _(sy, p=p, gsv=gsv):
                    sy.wait_ge(s_v, gsv)
                    evac_ms[("gsp", p)] = inc("p", sy.dma_start(out=d_gate[:, p * 1024:(p + 1) * 1024],
                                                                in_=gstage[:, 0:1024]), 16)

        p1_t, p1_v, p1_s, p1_p = C["t"], C["v"], C["s"], C["p"]

        seq = []
        for j in range(32):
            seq += [j, j + 32]

        @block.gpsimd
        def _(g):
            g.wait_ge(s_t, p1_t)
            inc("w", g.dma_start(out=wsb[:, 0:4096].rearrange("d (a q) -> d a q", q=128),
                                 in_=W12_b[seq[0]:seq[0] + 1].rearrange("b a d q -> b d a q")), 16)
            inc("w", g.dma_start(out=wsb[:, 4096:8192].rearrange("d (a q) -> d a q", q=128),
                                 in_=W12_b[seq[1]:seq[1] + 1].rearrange("b a d q -> b d a q")), 16)
        p1_w_units = C["w"] // 16

        ff_evac = {0: p1_v, (0, "s"): p1_s}
        for jp in range(32):
            @block.tensor
            def _(t, jp=jp):
                t.wait_ge(s_w, 16 * (p1_w_units - 2 + 2 * jp + 1))
                t.wait_ge(s_v, ff_evac[0])
                t.wait_ge(s_s, ff_evac[(0, "s")])
                for k in range(32):
                    lhs = wsb[:, k * 128:(k + 1) * 128]
                    inc("t", t.matmul(pmA[:, 0:512], lhs, arena[:, k * 1027:k * 1027 + 512],
                                      start=(k == 0), stop=(k == 31)))
                for k in range(32):
                    lhs = wsb[:, k * 128:(k + 1) * 128]
                    inc("t", t.matmul(pmB[:, 0:512], lhs, arena[:, k * 1027 + 512:k * 1027 + 1024],
                                      start=(k == 0), stop=(k == 31)))
            x1ms = C["t"]

            if jp + 1 < 32:
                @block.gpsimd
                def _(g, jp=jp, x1ms=x1ms):
                    g.wait_ge(s_t, x1ms)
                    inc("w", g.dma_start(out=wsb[:, 0:4096].rearrange("d (a q) -> d a q", q=128),
                                         in_=W12_b[seq[2 * jp + 2]:seq[2 * jp + 2] + 1].rearrange("b a d q -> b d a q")), 16)

            @block.tensor
            def _(t, jp=jp):
                t.wait_ge(s_w, 16 * (p1_w_units - 2 + 2 * jp + 2))
                for k in range(32):
                    lhs = wsb[:, 4096 + k * 128:4096 + (k + 1) * 128]
                    inc("t", t.matmul(pmC[:, 0:512], lhs, arena[:, k * 1027:k * 1027 + 512],
                                      start=(k == 0), stop=(k == 31)))
                for k in range(32):
                    lhs = wsb[:, 4096 + k * 128:4096 + (k + 1) * 128]
                    inc("t", t.matmul(pmD[:, 0:512], lhs, arena[:, k * 1027 + 512:k * 1027 + 1024],
                                      start=(k == 0), stop=(k == 31)))
            x2ms = C["t"]

            if jp + 1 < 32:
                @block.gpsimd
                def _(g, jp=jp, x2ms=x2ms):
                    g.wait_ge(s_t, x2ms)
                    inc("w", g.dma_start(out=wsb[:, 4096:8192].rearrange("d (a q) -> d a q", q=128),
                                         in_=W12_b[seq[2 * jp + 3]:seq[2 * jp + 3] + 1].rearrange("b a d q -> b d a q")), 16)

            @block.scalar
            def _(s, jp=jp, x1ms=x1ms):
                s.wait_ge(s_t, x1ms)
                s.wait_ge(s_v, ff_evac[0])
                inc("s", s.activation(silu_t[:, 0:512], pmA[:, 0:512], AF.Silu, bias=b12_sb[:, jp:jp + 1]))
                inc("s", s.activation(silu_t[:, 512:1024], pmB[:, 0:512], AF.Silu, bias=b12_sb[:, jp:jp + 1]))
            sls = C["s"]

            @block.vector
            def _(v, jp=jp, x2ms=x2ms, sls=sls):
                v.wait_ge(s_t, x2ms)
                v.wait_ge(s_s, sls)
                v.wait_ge(s_p, ff_evac.get(("x4sp", jp - 1), 0))
                inc("v", v.scalar_tensor_tensor(x4stage[:, 0:512], pmC[:, 0:512], b12_sb[:, 32 + jp:33 + jp],
                                                silu_t[:, 0:512], ALU.add, ALU.mult))
                inc("v", v.scalar_tensor_tensor(x4stage[:, 512:1024], pmD[:, 0:512], b12_sb[:, 32 + jp:33 + jp],
                                                silu_t[:, 512:1024], ALU.add, ALU.mult))
            ff_evac[0] = C["v"]
            ff_evac[(0, "s")] = C["s"]
            x4v = C["v"]

            @block.sync
            def _(sy, jp=jp, x4v=x4v):
                sy.wait_ge(s_v, x4v)
                ff_evac[("x4sp", jp)] = inc("p", sy.dma_start(out=d_x4[:, jp * 1024:(jp + 1) * 1024],
                                                              in_=x4stage[:, 0:1024]), 16)

        p1b_t, p1b_v, p1b_s = C["t"], C["v"], C["s"]

        @block.vector
        def _(v):
            v.wait_ge(s_t, p1b_t)
            inc("v", v.memset(srun[:, :], 0.0))
            inc("v", v.memset(sbf[:, :], 0.0))

        for c in range(NQ):
            @block.sync
            def _(sy, c=c):
                sy.wait_ge(s_t, C["t"])
                sy.wait_ge(s_p, p1_p)
                inc("l", sy.dma_start(out=wsb[:, 0:4096].rearrange("p (k t) -> p k t", t=128),
                                      in_=d_xs.rearrange("p (k u) -> p k u", u=1024)[:, :, c * 128:(c + 1) * 128]), 16)
            xchl = C["l"]

            for h in range(H):
                g = h // 8
                par = h % 2
                if h % 8 == 0:
                    @block.tensor
                    def _(t, g=g, c=c, xchl=xchl):
                        t.wait_ge(s_v, C["v"])
                        t.wait_ge(s_s, C["s"])
                        if h == 0 and g == 0:
                            t.wait_ge(s_l, xchl)
                        inc("t", t.matmul(pmS[:, 0:128], BC[:, g * 1024 + c * 128:g * 1024 + (c + 1) * 128],
                                          BC[:, (8 + g) * 1024 + c * 128:(8 + g) * 1024 + (c + 1) * 128],
                                          start=True, stop=True))
                        inc("t", t.transpose(pbT[:, 0:128], BC[:, g * 1024 + c * 128:g * 1024 + (c + 1) * 128],
                                             identb_s[:, :]))
                    gms = C["t"]

                    @block.scalar
                    def _(s, gms=gms):
                        s.wait_ge(s_t, gms)
                        inc("s", s.copy(G_bf[:, :], pmS[:, 0:128]))
                        inc("s", s.copy(BT_sb[:, :], pbT[:, 0:128]))

                if h % 4 == 0:
                    spar = (h // 4) % 2

                    @block.sync
                    def _(sy, h=h, c=c, spar=spar):
                        sy.wait_ge(s_t, C["t"])
                        inc("l", sy.dma_start(
                            out=stage[0:1, spar * 512:spar * 512 + 512],
                            in_=cum[h:h + 4, c * Q:(c + 1) * Q]), 16)
                    stl = C["l"]

                    @block.tensor
                    def _(t, spar=spar, stl=stl, xchl=xchl):
                        t.wait_ge(s_l, max(stl, xchl))
                        t.wait_ge(s_v, C["v"])
                        t.wait_ge(s_s, C["s"])
                        inc("t", t.matmul(pmA[:, 0:512], onesf[0:1, :], stage[0:1, spar * 512:spar * 512 + 512],
                                          start=True, stop=True))
                    ems = C["t"]

                    @block.vector
                    def _(v, h=h, c=c, ems=ems):
                        v.wait_ge(s_t, ems)
                        for q in range(4):
                            inc("v", v.scalar_tensor_tensor(e_sb[:, q * 128:(q + 1) * 128], pmA[:, q * 128:(q + 1) * 128],
                                                            cumT[:, c * 64 + h + q:c * 64 + h + q + 1], AM[:, :],
                                                            ALU.subtract, ALU.add))
                    ev = C["v"]

                    @block.scalar
                    def _(s, ev=ev):
                        s.wait_ge(s_v, ev)
                        inc("s", s.activation(Dm[:, 0:512], e_sb[:, 0:512], AF.Exp))

                @block.vector
                def _(v, h=h):
                    v.wait_ge(s_s, C["s"])
                    inc("v", v.tensor_tensor(Mp[:, (h % 2) * 128:(h % 2) * 128 + 128],
                                             Dm[:, (h % 4) * 128:(h % 4) * 128 + 128], G_bf[:, :], ALU.mult))
                mpv = C["v"]

                @block.tensor
                def _(t, h=h, par=par):
                    t.wait_ge(s_s, C["s"])
                    bb = (h % 2) * 64
                    inc("t", t.transpose(pbT[:, 128 + par * 64:128 + par * 64 + 64],
                                         wsb[bb:bb + 64, (h // 2) * 128:(h // 2) * 128 + 128],
                                         identb_s[bb:bb + 64, bb:bb + 64]))
                xtm = C["t"]

                @block.scalar
                def _(s, h=h, par=par, c=c, xtm=xtm):
                    s.wait_ge(s_t, xtm)
                    s.wait_ge(s_v, C["v"])
                    inc("s", s.copy(xts[:, par * 64:par * 64 + 64], pbT[:, 128 + par * 64:128 + par * 64 + 64]))
                    inc("s", s.activation(colb[:, par * 2:par * 2 + 1], cumT[:, c * 64 + h:c * 64 + h + 1], AF.Exp))
                    inc("s", s.activation(colb[:, par * 2 + 1:par * 2 + 2], cumT[:, c * 64 + h:c * 64 + h + 1],
                                          AF.Exp, scale=-1.0, bias=ends_rep[:, h * 8 + c:h * 8 + c + 1]))
                xtc = C["s"]

                @block.vector
                def _(v, h=h, par=par, c=c, xtc=xtc):
                    v.wait_ge(s_s, xtc)
                    inc("v", v.tensor_scalar(dtx[:, par * 64:par * 64 + 64], xts[:, par * 64:par * 64 + 64],
                                             dtpT[:, c * 64 + h:c * 64 + h + 1], None, ALU.mult))
                    inc("v", v.tensor_scalar(dtxs[:, par * 64:par * 64 + 64], dtx[:, par * 64:par * 64 + 64],
                                             colb[:, par * 2 + 1:par * 2 + 2], None, ALU.mult))
                dxv = C["v"]

                @block.tensor
                def _(t, h=h, g=g, c=c, par=par, dxv=dxv):
                    t.wait_ge(s_v, dxv)
                    inc("t", t.matmul(pmC[:, par * 64:par * 64 + 64], Mp[:, (h % 2) * 128:(h % 2) * 128 + 128],
                                      dtx[:, par * 64:par * 64 + 64], start=True, stop=True))
                    inc("t", t.matmul(pmC[:, 128 + par * 64:128 + par * 64 + 64],
                                      BC[:, (8 + g) * 1024 + c * 128:(8 + g) * 1024 + (c + 1) * 128],
                                      sbf[:, h * 64:(h + 1) * 64], start=True, stop=True))
                    inc("t", t.matmul(pmC[:, 256 + par * 64:256 + par * 64 + 64], BT_sb[:, :],
                                      dtxs[:, par * 64:par * 64 + 64], start=True, stop=True))
                yms = C["t"]

                @block.vector
                def _(v, h=h, c=c, par=par, yms=yms):
                    v.wait_ge(s_t, yms)
                    inc("v", v.tensor_scalar(yg1[:, par * 64:par * 64 + 64], pmC[:, 128 + par * 64:128 + par * 64 + 64],
                                             colb[:, par * 2:par * 2 + 1], None, ALU.mult))
                    inc("v", v.scalar_tensor_tensor(yg2[:, par * 64:par * 64 + 64], xts[:, par * 64:par * 64 + 64],
                                                    D_rep[:, h:h + 1], pmC[:, par * 64:par * 64 + 64],
                                                    ALU.mult, ALU.add))
                    inc("v", v.tensor_tensor(arena[:, c * 4096 + h * 64:c * 4096 + (h + 1) * 64],
                                             yg1[:, par * 64:par * 64 + 64], yg2[:, par * 64:par * 64 + 64], ALU.add))
                    inc("v", v.tensor_scalar(yg1[:, par * 64:par * 64 + 64], srun[:, h * 64:(h + 1) * 64],
                                             dq_rep[:, h * 8 + c:h * 8 + c + 1], None, ALU.mult))
                    inc("v", v.tensor_tensor(srun[:, h * 64:(h + 1) * 64], yg1[:, par * 64:par * 64 + 64],
                                             pmC[:, 256 + par * 64:256 + par * 64 + 64], ALU.add))

            @block.scalar
            def _(s, c=c):
                s.wait_ge(s_v, C["v"])
                inc("s", s.copy(sbf[:, :], srun[:, :]))

        p2_t, p2_v, p2_s = C["t"], C["v"], C["s"]

        @block.sync
        def _(sy):
            sy.wait_ge(s_s, p2_s)
            sy.wait_ge(s_v, p2_v)
            inc("p", sy.dma_start(out=cc_in.ap()[:, 0:4096], in_=srun[:, :]), 16)
            inc("p", sy.dma_start(out=cc_in.ap()[0:64, 4096:4097], in_=scanned[:, NQ - 1:NQ],
                                  allow_slow_non_contiguous=True), 16)
        blobp = C["p"]

        @block.gpsimd
        def _(g):
            g.wait_ge(s_p, blobp)
            inc("cc", g.collective_compute("AllGather", ALU.bypass, replica_groups=[list(range(NCORE))],
                                           ins=[cc_in.ap()], outs=[cc_out.ap()]))

        @block.sync
        def _(sy):
            sy.wait_ge(s_cc, C["cc"])
            for cp in range(NCORE):
                inc("l", sy.dma_start(out=CEall[:, cp:cp + 1], in_=cc_out.ap()[cp * 128:cp * 128 + 64, 4096:4097],
                                      allow_slow_non_contiguous=True), 16)
        cel = C["l"]

        @block.vector
        def _(v, cel=cel):
            v.wait_ge(s_l, cel)
            inc("v", v.memset(srun[:, :], 0.0))
        w8v = C["v"]

        @block.sync
        def _(sy, cel=cel, w8v=w8v):
            sy.wait_ge(s_l, cel)
            sy.wait_ge(s_t, C["t"])
            inc("l", sy.dma_start(out=stage[0:1, 0:512], in_=CEall[:, 0:NCORE]), 16)
        w8l = C["l"]

        @block.tensor
        def _(t, w8l=w8l):
            t.wait_ge(s_l, w8l)
            t.wait_ge(s_v, C["v"])
            t.wait_ge(s_s, C["s"])
            inc("t", t.matmul(pmS[:, 0:512], onesf[0:1, :], stage[0:1, 0:512], start=True, stop=True))
        w8m = C["t"]

        @block.vector
        def _(v, w8m=w8m):
            v.wait_ge(s_t, w8m)
            inc("v", v.tensor_tensor(W8_rep[:, 0:512], pmS[:, 0:512], M1sb[:, 0:512], ALU.mult))
        w8vb = C["v"]

        @block.scalar
        def _(s, w8vb=w8vb):
            s.wait_ge(s_v, w8vb)
            inc("s", s.activation(ends_rep[:, 0:512], W8_rep[:, 0:512], AF.Exp))
        w8s = C["s"]

        for cp in range(NCORE):
            @block.sync
            def _(sy, cp=cp, w8s=w8s):
                sy.wait_ge(s_v, C["v"])
                sy.wait_ge(s_s, w8s)
                inc("l", sy.dma_start(out=f32a[:, :], in_=cc_out.ap()[cp * 128:(cp + 1) * 128, 0:4096]), 16)
            sgl = C["l"]

            @block.vector
            def _(v, cp=cp, sgl=sgl):
                v.wait_ge(s_l, sgl)
                v.wait_ge(s_s, C["s"])
                for h in range(H):
                    inc("v", v.tensor_scalar(srun[:, h * 64:(h + 1) * 64], srun[:, h * 64:(h + 1) * 64],
                                             ends_rep[:, h * 8 + cp:h * 8 + cp + 1], None, ALU.mult))
                    inc("v", v.tensor_scalar(yg1[:, 0:64], f32a[:, h * 64:(h + 1) * 64],
                                             M1sb[:, h * 8 + cp:h * 8 + cp + 1], None, ALU.mult))
                    inc("v", v.tensor_tensor(srun[:, h * 64:(h + 1) * 64], srun[:, h * 64:(h + 1) * 64],
                                             yg1[:, 0:64], ALU.add))
        sinv = C["v"]

        @block.scalar
        def _(s, sinv=sinv):
            s.wait_ge(s_v, sinv)
            inc("s", s.copy(sbf[:, :], srun[:, :]))
        sinbf = C["s"]

        for c in range(NQ):
            for h in range(H):
                g = h // 8
                par = h % 2

                @block.scalar
                def _(s, h=h, c=c, par=par, sinbf=sinbf):
                    s.wait_ge(s_s, sinbf)
                    s.wait_ge(s_v, C["v"])
                    inc("s", s.activation(colb[:, 4 + par:5 + par], cumT[:, c * 64 + h:c * 64 + h + 1], AF.Exp,
                                          bias=pref_rep[:, h * 8 + c:h * 8 + c + 1]))
                ecs = C["s"]

                @block.tensor
                def _(t, h=h, g=g, c=c, par=par, ecs=ecs):
                    t.wait_ge(s_s, ecs)
                    t.wait_ge(s_v, C["v"])
                    inc("t", t.matmul(pmC[:, par * 64:par * 64 + 64],
                                      BC[:, (8 + g) * 1024 + c * 128:(8 + g) * 1024 + (c + 1) * 128],
                                      sbf[:, h * 64:(h + 1) * 64], start=True, stop=True))
                ycm = C["t"]

                @block.vector
                def _(v, h=h, c=c, par=par, ycm=ycm):
                    v.wait_ge(s_t, ycm)
                    inc("v", v.tensor_scalar(yg1[:, par * 64:par * 64 + 64], pmC[:, par * 64:par * 64 + 64],
                                             colb[:, 4 + par:5 + par], None, ALU.mult))
                    inc("v", v.tensor_tensor(arena[:, c * 4096 + h * 64:c * 4096 + (h + 1) * 64],
                                             arena[:, c * 4096 + h * 64:c * 4096 + (h + 1) * 64],
                                             yg1[:, par * 64:par * 64 + 64], ALU.add))

        p4_t, p4_v, p4_s = C["t"], C["v"], C["s"]

        for c in range(NQ):
            for g in range(GD):
                @block.scalar
                def _(s, c=c, g=g):
                    if c == 0 and g == 0:
                        s.wait_ge(s_v, p4_v)
                    inc("s", s.activation(f32a[:, 0:512], arena[:, c * 4096 + g * 512:c * 4096 + (g + 1) * 512],
                                          AF.Square, accum_out=ss_sb[:, c * 8 + g:c * 8 + g + 1]))
        sqs = C["s"]

        @block.vector
        def _(v, sqs=sqs):
            v.wait_ge(s_s, sqs)
            inc("v", v.tensor_scalar(rg_sb[:, 0:64], ss_sb[:, 0:64], 1.0 / 512.0, 1e-5, ALU.mult, ALU.add))
        rgv = C["v"]

        @block.scalar
        def _(s, rgv=rgv):
            s.wait_ge(s_v, rgv)
            inc("s", s.activation(rg_sb[:, 0:64], rg_sb[:, 0:64], AF.Ln))
            inc("s", s.activation(rg_sb[:, 0:64], rg_sb[:, 0:64], AF.Exp, scale=-0.5))
        rgs = C["s"]

        @block.vector
        def _(v, rgs=rgs):
            v.wait_ge(s_s, rgs)
            for c in range(NQ):
                for g in range(GD):
                    inc("v", v.tensor_scalar(arena[:, c * 4096 + g * 512:c * 4096 + (g + 1) * 512],
                                             arena[:, c * 4096 + g * 512:c * 4096 + (g + 1) * 512],
                                             rg_sb[:, c * 8 + g:c * 8 + g + 1], None, ALU.mult))
        rnv = C["v"]

        last_yn = {0: 0, 1: 0}
        for khalf in range(2):
            @block.sync
            def _(sy, khalf=khalf, rnv=rnv):
                sy.wait_ge(s_v, rnv)
                sy.wait_ge(s_t, C["t"])
                sy.wait_ge(s_p, p1_p)
                inc("l", sy.dma_start(out=BC[:, 0:16384], in_=d_gate[:, khalf * 16384:(khalf + 1) * 16384]), 16)
            gl = C["l"]

            for k in range(khalf * 16, khalf * 16 + 16):
                for c in range(NQ):
                    par = c % 2

                    @block.tensor
                    def _(t, k=k, c=c, par=par, gl=gl):
                        t.wait_ge(s_l, gl)
                        t.wait_ge(s_v, C["v"])
                        inc("t", t.transpose(pbT[:, par * 128:par * 128 + 128],
                                             arena[:, c * 4096 + k * 128:c * 4096 + (k + 1) * 128],
                                             identb_s[:, :]))
                    ytm = C["t"]

                    @block.vector
                    def _(v, k=k, c=c, par=par, khalf=khalf, ytm=ytm):
                        v.wait_ge(s_t, ytm)
                        v.wait_ge(s_p, last_yn[par])
                        inc("v", v.tensor_tensor(ynstage[:, par * 128:par * 128 + 128],
                                                 pbT[:, par * 128:par * 128 + 128],
                                                 BC[:, (k - khalf * 16) * 1024 + c * 128:(k - khalf * 16) * 1024 + (c + 1) * 128],
                                                 ALU.mult))
                    ynv = C["v"]

                    @block.sync
                    def _(sy, k=k, c=c, par=par, ynv=ynv):
                        sy.wait_ge(s_v, ynv)
                        last_yn[par] = inc("p", sy.dma_start(out=d_yn[:, k * 1024 + c * 128:k * 1024 + (c + 1) * 128],
                                                             in_=ynstage[:, par * 128:par * 128 + 128]), 16)

        for tk in range(2):
            @block.sync
            def _(sy, tk=tk):
                sy.wait_ge(s_p, C["p"])
                sy.wait_ge(s_t, C["t"])
                sy.wait_ge(s_s, C["s"])
                inc("l", sy.dma_start(out=arena[:, 0:16384].rearrange("p (k t) -> p k t", t=512),
                                      in_=d_yn.rearrange("p (k u) -> p k u", u=1024)[:, :, tk * 512:tk * 512 + 512]), 16)
                inc("l", sy.dma_start(out=arena[:, 16384:32768].rearrange("p (k t) -> p k t", t=512),
                                      in_=d_x4.rearrange("p (k u) -> p k u", u=1024)[:, :, tk * 512:tk * 512 + 512]), 16)
            rhl = C["l"]

            slabbuf = [wsb, BC]
            w_units = {}

            @block.gpsimd
            def _(g, tk=tk):
                g.wait_ge(s_t, C["t"])
                g.wait_ge(s_v, C["v"])
                inc("w", g.dma_start(out=wsb[:, 0:4096].rearrange("d (a q) -> d a q", q=128),
                                     in_=Wout_b[0:1].rearrange("b a d q -> b d a q")), 16)
                inc("w", g.dma_start(out=wsb[:, 4096:8192].rearrange("d (a q) -> d a q", q=128),
                                     in_=W3_b[0:1].rearrange("b a d q -> b d a q")), 16)
                inc("w", g.dma_start(out=BC[:, 0:4096].rearrange("d (a q) -> d a q", q=128),
                                     in_=Wout_b[1:2].rearrange("b a d q -> b d a q")), 16)
                inc("w", g.dma_start(out=BC[:, 4096:8192].rearrange("d (a q) -> d a q", q=128),
                                     in_=W3_b[1:2].rearrange("b a d q -> b d a q")), 16)
            w_units[0] = C["w"] - 32
            w_units[1] = C["w"]
            oms_hist = {}

            o_evac = {0: C["v"], 1: C["v"]}
            for do in range(32):
                par = do % 2
                pmo = pmA if par == 0 else pmB
                sb = slabbuf[par]

                @block.tensor
                def _(t, do=do, pmo=pmo, par=par, rhl=rhl, sb=sb):
                    t.wait_ge(s_l, rhl)
                    t.wait_ge(s_w, w_units[do])
                    t.wait_ge(s_v, o_evac[par])
                    for k in range(32):
                        inc("t", t.matmul(pmo[:, 0:512], sb[:, k * 128:(k + 1) * 128],
                                          arena[:, k * 512:(k + 1) * 512], start=(k == 0), stop=False))
                    for k in range(32):
                        inc("t", t.matmul(pmo[:, 0:512], sb[:, 4096 + k * 128:4096 + (k + 1) * 128],
                                          arena[:, 16384 + k * 512:16384 + (k + 1) * 512],
                                          start=False, stop=(k == 31)))
                oms = C["t"]
                oms_hist[do] = oms

                if do + 2 < 32:
                    @block.gpsimd
                    def _(g, do=do, par=par, sb=sb):
                        g.wait_ge(s_t, oms_hist[do])
                        inc("w", g.dma_start(out=sb[:, 0:4096].rearrange("d (a q) -> d a q", q=128),
                                             in_=Wout_b[do + 2:do + 3].rearrange("b a d q -> b d a q")), 16)
                        inc("w", g.dma_start(out=sb[:, 4096:8192].rearrange("d (a q) -> d a q", q=128),
                                             in_=W3_b[do + 2:do + 3].rearrange("b a d q -> b d a q")), 16)
                    w_units[do + 2] = C["w"]

                @block.sync
                def _(sy, do=do, tk=tk, par=par):
                    sy.wait_ge(s_v, o_evac.get((par, "xt"), 0))
                    inc("l", sy.dma_start(out=f32a[:, par * 512:par * 512 + 512],
                                          in_=xT_loc[do * 128:(do + 1) * 128, tk * 512:tk * 512 + 512]), 16)
                xtl = C["l"]

                @block.vector
                def _(v, do=do, pmo=pmo, par=par, oms=oms, xtl=xtl):
                    v.wait_ge(s_t, oms)
                    v.wait_ge(s_l, xtl)
                    v.wait_ge(s_o, o_evac.get((par, "op"), 0))
                    inc("v", v.scalar_tensor_tensor(f32a[:, 2048 + par * 512:2048 + par * 512 + 512],
                                                    pmo[:, 0:512], b3_sb[:, do:do + 1],
                                                    f32a[:, par * 512:par * 512 + 512], ALU.add, ALU.add))
                o_evac[par] = C["v"]
                o_evac[(par, "xt")] = C["v"]
                ov = C["v"]

                @block.sync
                def _(sy, do=do, tk=tk, par=par, ov=ov):
                    sy.wait_ge(s_v, ov)
                    o_evac[(par, "op")] = inc("o", sy.dma_start(
                        out=outT[do * 128:(do + 1) * 128, tk * 512:tk * 512 + 512],
                        in_=f32a[:, 2048 + par * 512:2048 + par * 512 + 512]), 16)

        @block.sync
        def _(sy):
            sy.wait_ge(s_o, C["o"])

    return nc


_NC_CACHE = [None]


def _host_prep(inp):
    f32 = np.float32
    x = np.asarray(inp["L0_attn_hidden_states"], f32).reshape(L, DM)
    Win = np.asarray(inp["L0_attn_in_proj_w"], f32)
    Wout = np.asarray(inp["L0_attn_out_proj_w"], f32)
    W12 = np.asarray(inp["L0_ffn_w12"], f32)
    W3 = np.asarray(inp["L0_ffn_w3"], f32)

    def blockT(W, npt):
        out_dim = W.shape[0]
        pad = npt * 128 - out_dim
        if pad:
            W = np.concatenate([W, np.zeros((pad, W.shape[1]), f32)], axis=0)
        WT = W.T
        B = WT.reshape(32, 128, npt, 128).transpose(2, 0, 1, 3)
        return np.ascontiguousarray(B)

    Win_b = blockT(Win, NPT)
    Wout_b = blockT(Wout, 32)
    W12_b = blockT(W12, 64)
    W3_b = blockT(W3, 32)
    xT = np.ascontiguousarray(x.T)

    identb = np.eye(128, dtype=ml_dtypes.bfloat16)
    identf = np.eye(128, dtype=f32)
    t_i = np.arange(128)
    AM = np.where(t_i[None, :] >= t_i[:, None], 0.0, -1e30).astype(f32)

    maps = []
    for c in range(NCORE):
        t0 = c * LC
        halo = x[t0 - 3:t0] if c > 0 else np.zeros((3, DM), f32)
        m1row = np.zeros((1, H * NCORE), f32)
        for hh in range(H):
            for jj in range(NCORE):
                if jj < c:
                    m1row[0, hh * NCORE + jj] = 1.0
        M1full = np.ascontiguousarray(np.repeat(m1row, 128, axis=0))
        maps.append({
            "x_loc": np.ascontiguousarray(x[t0:t0 + LC]),
            "x_halo": np.ascontiguousarray(halo),
            "xT_loc": np.ascontiguousarray(xT[:, t0:t0 + LC]),
            "Win_b": Win_b, "Wout_b": Wout_b, "W12_b": W12_b, "W3_b": W3_b,
            "conv_w": np.asarray(inp["L0_attn_conv_weight"], f32),
            "conv_b": np.asarray(inp["L0_attn_conv_bias"], f32),
            "dt_bias": np.asarray(inp["L0_attn_dt_bias"], f32),
            "A_log": np.asarray(inp["L0_attn_A_log"], f32),
            "D_in": np.asarray(inp["L0_attn_D"], f32),
            "nw_in": np.asarray(inp["L0_attn_norm_weight"], f32),
            "b12_in": np.asarray(inp["L0_ffn_b12"], f32),
            "b3_in": np.asarray(inp["L0_ffn_b3"], f32),
            "identb": identb, "identf": identf, "AM_in": AM,
            "M1_in": M1full,
        })
    return maps


def kernel(**inp):
    maps = _host_prep(inp)
    if _NC_CACHE[0] is None:
        _NC_CACHE[0] = build_nc()
    nc = _NC_CACHE[0]
    res = run_bass_kernel_spmd(nc, maps, core_ids=list(range(NCORE)))
    outs = [np.asarray(res.results[c]["outT"]) for c in range(NCORE)]
    outT = np.concatenate(outs, axis=1)
    return np.ascontiguousarray(outT.T).reshape(1, L, DM).astype(np.float32)


if __name__ == "__main__":
    import time
    t0 = time.time()
    nc = build_nc()
    print("build ok", time.time() - t0, "s")


# version 06 (final)
# speedup vs baseline: 1.1562x; 1.0585x over previous
import numpy as np
import ml_dtypes
import concourse.bass as bass
import concourse.mybir as mybir
from concourse.bass_utils import run_bass_kernel_spmd
from contextlib import ExitStack

F32 = mybir.dt.float32
BF16 = mybir.dt.bfloat16
AF = mybir.ActivationFunctionType
ALU = mybir.AluOpType
AXX = mybir.AxisListType.X

H = 64; PD = 64; ND = 128; GD = 8; KC = 4
DM = 4096; ID = 4096; GN = 1024; CONV = 6144; PROJ = 10304
L = 8192; NCORE = 8; LC = 1024; Q = 128; NQ = 8
NPT = 81
HBC0, HBC1 = 32, 80
DTT = 80


def build_nc():
    nc = bass.Bass("TRN2", target_bir_lowering=False)
    def din(name, shape, dt=F32):
        return nc.dram_tensor(name, shape, dt, kind="ExternalInput").ap()

    x_loc = din("x_loc", [LC, DM])
    x_halo = din("x_halo", [3, DM])
    xT_loc = din("xT_loc", [DM, LC])
    Win_b = din("Win_b", [NPT, 32, 128, 128])
    Wout_b = din("Wout_b", [32, 32, 128, 128])
    W12_b = din("W12_b", [64, 32, 128, 128])
    W3_b = din("W3_b", [32, 32, 128, 128])
    conv_w = din("conv_w", [CONV, KC])
    conv_b = din("conv_b", [CONV])
    dt_bias = din("dt_bias", [H])
    A_log = din("A_log", [H])
    D_in = din("D_in", [H])
    nw_in = din("nw_in", [ID])
    b12_in = din("b12_in", [2 * DM])
    b3_in = din("b3_in", [DM])
    identb = din("identb", [128, 128], BF16)
    identf = din("identf", [128, 128])
    AM_in = din("AM_in", [128, 128])
    M1_in = din("M1_in", [128, H * NCORE])

    outT = nc.dram_tensor("outT", [DM, LC], F32, kind="ExternalOutput").ap()

    d_gate = nc.dram_tensor("d_gate", [128, 32 * 1024], BF16).ap()
    d_xs = nc.dram_tensor("d_xs", [128, 32 * 1024], BF16).ap()
    d_x4 = nc.dram_tensor("d_x4", [128, 32 * 1024], BF16).ap()
    d_yn = nc.dram_tensor("d_yn", [128, 32 * 1024], BF16).ap()
    cc_in = nc.dram_tensor("cc_in", [128, 4097], F32)
    cc_out = nc.dram_tensor("cc_out", [128 * NCORE, 4097], F32)

    C = {"t": 0, "v": 0, "s": 0, "w": 0, "l": 0, "p": 0, "o": 0, "cc": 0}

    with ExitStack() as ctx:
        E = ctx.enter_context
        arena = E(nc.sbuf_tensor("arena", [128, 32 * 1027], BF16))
        BC = E(nc.sbuf_tensor("BC", [128, 16384], BF16))
        wsb = E(nc.sbuf_tensor("wsb", [128, 8192], BF16))
        f32a = E(nc.sbuf_tensor("f32a", [128, 4096], F32))
        srun = E(nc.sbuf_tensor("srun", [128, 4096], F32))
        sbf = E(nc.sbuf_tensor("sbf", [128, 4096], BF16))
        cum = E(nc.sbuf_tensor("cum", [H, LC], F32))
        dtp = E(nc.sbuf_tensor("dtp", [H, LC], F32))
        dtsb = E(nc.sbuf_tensor("dtsb", [H, LC], F32))
        dt64 = E(nc.sbuf_tensor("dt64", [H, LC], F32))
        cumT = E(nc.sbuf_tensor("cumT", [128, 512], F32))
        dtpT = E(nc.sbuf_tensor("dtpT", [128, 512], F32))
        ends_rep = E(nc.sbuf_tensor("ends_rep", [128, 512], F32))
        dq_rep = E(nc.sbuf_tensor("dq_rep", [128, 512], F32))
        pref_rep = E(nc.sbuf_tensor("pref_rep", [128, 512], F32))
        W8_rep = E(nc.sbuf_tensor("W8_rep", [128, 512], F32))
        D_rep = E(nc.sbuf_tensor("D_rep", [128, 64], F32))
        Drow = E(nc.sbuf_tensor("Drow", [1, 64], F32))
        stage = E(nc.sbuf_tensor("stage", [1, 1024], F32))
        onesf = E(nc.sbuf_tensor("onesf", [1, 128], F32))
        ones64 = E(nc.sbuf_tensor("ones64", [H, 128], F32))
        identb_s = E(nc.sbuf_tensor("identb_s", [128, 128], BF16))
        identf_s = E(nc.sbuf_tensor("identf_s", [128, 128], F32))
        AM = E(nc.sbuf_tensor("AM", [128, 128], F32))
        M1sb = E(nc.sbuf_tensor("M1sb", [128, H * NCORE], F32))
        convw_sb = E(nc.sbuf_tensor("convw_sb", [128, 192], F32))
        convb_sb = E(nc.sbuf_tensor("convb_sb", [128, 48], F32))
        dtb_col = E(nc.sbuf_tensor("dtb_col", [H, 1], F32))
        Alog_col = E(nc.sbuf_tensor("Alog_col", [H, 1], F32))
        Acol = E(nc.sbuf_tensor("Acol", [H, 1], F32))
        nw_sb = E(nc.sbuf_tensor("nw_sb", [128, 32], F32))
        b12_sb = E(nc.sbuf_tensor("b12_sb", [128, 64], F32))
        b3_sb = E(nc.sbuf_tensor("b3_sb", [128, 32], F32))
        ends = E(nc.sbuf_tensor("ends", [H, NQ], F32))
        scanned = E(nc.sbuf_tensor("scanned", [H, NQ], F32))
        pref = E(nc.sbuf_tensor("pref", [H, NQ], F32))
        CEall = E(nc.sbuf_tensor("CEall", [H, NCORE], F32))
        Ssum = E(nc.sbuf_tensor("Ssum", [H, NCORE], F32))
        Sme = E(nc.sbuf_tensor("Sme", [H, 1], F32))
        tmp8 = E(nc.sbuf_tensor("tmp8", [H, NCORE], F32))
        lncol = E(nc.sbuf_tensor("lncol", [128, 8], F32))
        G_bf = E(nc.sbuf_tensor("G_bf", [128, 128], BF16))
        BT_sb = E(nc.sbuf_tensor("BT_sb", [128, 128], BF16))
        Dm = E(nc.sbuf_tensor("Dm", [128, 512], BF16))
        e_sb = E(nc.sbuf_tensor("e_sb", [128, 512], F32))
        Mp = E(nc.sbuf_tensor("Mp", [128, 256], BF16))
        xts = E(nc.sbuf_tensor("xts", [128, 128], BF16))
        dtx = E(nc.sbuf_tensor("dtx", [128, 128], BF16))
        dtxs = E(nc.sbuf_tensor("dtxs", [128, 128], BF16))
        colb = E(nc.sbuf_tensor("colb", [128, 8], F32))
        yg1 = E(nc.sbuf_tensor("yg1", [128, 128], F32))
        yg2 = E(nc.sbuf_tensor("yg2", [128, 128], F32))
        gstage = E(nc.sbuf_tensor("gstage", [128, 1024], BF16))
        xstage = E(nc.sbuf_tensor("xstage", [128, 1024], BF16))
        x4stage = E(nc.sbuf_tensor("x4stage", [128, 1024], BF16))
        silu_t = E(nc.sbuf_tensor("silu_t", [128, 1024], BF16))
        ynstage = E(nc.sbuf_tensor("ynstage", [128, 256], BF16))
        ss_sb = E(nc.sbuf_tensor("ss_sb", [128, 64], F32))
        rg_sb = E(nc.sbuf_tensor("rg_sb", [128, 64], F32))

        pmA = E(nc.psum_tensor("pmA", [128, 512], F32))
        pmB = E(nc.psum_tensor("pmB", [128, 512], F32))
        pmC = E(nc.psum_tensor("pmC", [128, 512], F32))
        pmD = E(nc.psum_tensor("pmD", [128, 512], F32))
        pmS = E(nc.psum_tensor("pmS", [128, 512], F32))
        pbT = E(nc.psum_tensor("pbT", [128, 512], BF16))
        pmH = E(nc.psum_tensor("pmH", [128, 8], F32))

        s_t = E(nc.semaphore("s_t")); s_v = E(nc.semaphore("s_v"))
        s_s = E(nc.semaphore("s_s")); s_w = E(nc.semaphore("s_w"))
        s_l = E(nc.semaphore("s_l")); s_p = E(nc.semaphore("s_p"))
        s_o = E(nc.semaphore("s_o")); s_cc = E(nc.semaphore("s_cc"))
        SEM = {"t": s_t, "v": s_v, "s": s_s, "w": s_w, "l": s_l, "p": s_p, "o": s_o, "cc": s_cc}
        block = E(nc.Block())

        def inc(key, instr, n=1):
            instr.then_inc(SEM[key], n)
            C[key] += n
            return C[key]

        @block.sync
        def _(sy):
            with nc.allow_non_contiguous_dma(reason="small one-time const loads"):
                inc("l", sy.dma_start(out=identb_s[:], in_=identb), 16)
                inc("l", sy.dma_start(out=identf_s[:], in_=identf), 16)
                inc("l", sy.dma_start(out=AM[:], in_=AM_in), 16)
                inc("l", sy.dma_start(out=M1sb[:], in_=M1_in), 16)
                inc("l", sy.dma_start(out=convb_sb[:], in_=conv_b.rearrange("(a j p) -> a p j", a=1, p=128)), 16)
                inc("l", sy.dma_start(out=convw_sb[:].rearrange("p (j k) -> p j k", k=KC),
                                      in_=conv_w.rearrange("(j p) k -> p j k", p=128)), 16)
                inc("l", sy.dma_start(out=dtb_col[:], in_=dt_bias.rearrange("(p b) -> p b", b=1)), 16)
                inc("l", sy.dma_start(out=Alog_col[:], in_=A_log.rearrange("(p b) -> p b", b=1)), 16)
                inc("l", sy.dma_start(out=Drow[:], in_=D_in.rearrange("(a k) -> a k", a=1)), 16)
                inc("l", sy.dma_start(out=nw_sb[:], in_=nw_in.rearrange("(a j p) -> a p j", a=1, p=128)), 16)
                inc("l", sy.dma_start(out=b12_sb[:], in_=b12_in.rearrange("(a j p) -> a p j", a=1, p=128)), 16)
                inc("l", sy.dma_start(out=b3_sb[:], in_=b3_in.rearrange("(a j p) -> a p j", a=1, p=128)), 16)
        const_l = C["l"]

        @block.vector
        def _(v):
            v.wait_ge(s_l, const_l)
            inc("v", v.memset(onesf[:], 1.0))
            inc("v", v.memset(ones64[:], 1.0))

        for j in range(9):
            npart = 128 if j < 8 else 3
            src = x_loc[j * 128:(j + 1) * 128, :] if j < 8 else x_halo
            ld_ms = [0]

            @block.sync
            def _(sy, src=src, npart=npart, ld_ms=ld_ms, j=j):
                if j > 0:
                    sy.wait_ge(s_v, C["v"])
                ld_ms[0] = inc("l", sy.dma_start(out=srun[0:npart, :], in_=src), 16)

            @block.scalar
            def _(s, npart=npart, ld_ms=ld_ms):
                s.wait_ge(s_l, ld_ms[0])
                s.wait_ge(s_v, C["v"])
                inc("s", s.activation(f32a[0:npart, :], srun[0:npart, :], AF.Identity, accum_out=lncol[0:npart, 0:1]))
                inc("s", s.activation(f32a[0:npart, :], srun[0:npart, :], AF.Square, accum_out=lncol[0:npart, 1:2]))
            ssq = C["s"]

            @block.vector
            def _(v, npart=npart, ssq=ssq):
                v.wait_ge(s_s, ssq)
                inc("v", v.tensor_scalar(lncol[0:npart, 2:3], lncol[0:npart, 0:1], 1.0 / DM, None, ALU.mult))
                inc("v", v.tensor_scalar(lncol[0:npart, 4:5], lncol[0:npart, 1:2], 1.0 / DM, 1e-5, ALU.mult, ALU.add))
            vvar = C["v"]

            @block.scalar
            def _(s, npart=npart, vvar=vvar):
                s.wait_ge(s_v, vvar)
                inc("s", s.activation(lncol[0:npart, 3:4], lncol[0:npart, 2:3], AF.Identity, scale=-1.0))
            sneg = C["s"]

            @block.vector
            def _(v, npart=npart, sneg=sneg):
                v.wait_ge(s_s, sneg)
                inc("v", v.scalar_tensor_tensor(lncol[0:npart, 5:6], lncol[0:npart, 2:3], lncol[0:npart, 3:4],
                                                lncol[0:npart, 4:5], ALU.mult, ALU.add))
            vve = C["v"]

            @block.scalar
            def _(s, npart=npart, vve=vve):
                s.wait_ge(s_v, vve)
                inc("s", s.activation(lncol[0:npart, 6:7], lncol[0:npart, 5:6], AF.Ln))
                inc("s", s.activation(lncol[0:npart, 6:7], lncol[0:npart, 6:7], AF.Exp, scale=-0.5))
            sr = C["s"]

            @block.vector
            def _(v, npart=npart, sr=sr):
                v.wait_ge(s_s, sr)
                v.wait_ge(s_t, C["t"])
                inc("v", v.tensor_scalar(f32a[0:npart, :], srun[0:npart, :], lncol[0:npart, 3:4], None, ALU.add))
                inc("v", v.tensor_scalar(BC[0:npart, 0:4096], f32a[0:npart, :], lncol[0:npart, 6:7], None, ALU.mult))
            vxn = C["v"]

            for kg in range(8):
                @block.tensor
                def _(t, kg=kg, npart=npart, vxn=vxn):
                    t.wait_ge(s_v, vxn)
                    if kg == 0:
                        t.wait_ge(s_l, const_l)
                    t.wait_ge(s_s, C["s"])
                    for q in range(4):
                        k = kg * 4 + q
                        inc("t", t.transpose(pbT[0:128, q * 128:q * 128 + npart],
                                             BC[0:npart, k * 128:(k + 1) * 128], identb_s[0:npart, 0:npart]))
                tms = C["t"]

                @block.scalar
                def _(s, kg=kg, npart=npart, j=j, tms=tms):
                    s.wait_ge(s_t, tms)
                    for q in range(4):
                        k = kg * 4 + q
                        off = k * 1027 + (j * 128 if j < 8 else 1024)
                        inc("s", s.copy(arena[:, off:off + npart], pbT[0:128, q * 128:q * 128 + npart]))

        p0_done_s = C["s"]
        p0_done_v = C["v"]

        order = [DTT] + list(range(HBC0, HBC1)) + list(range(0, 32))
        slab_consumed = {}
        evac_ms = {0: 0, 1: 0, (0, "s"): 0, (1, "s"): 0}

        @block.gpsimd
        def _(g):
            inc("w", g.dma_start(out=wsb[:, 0:4096].rearrange("d (a q) -> d a q", q=128),
                                 in_=Win_b[order[0]:order[0] + 1].rearrange("b a d q -> b d a q")), 16)
            inc("w", g.dma_start(out=wsb[:, 4096:8192].rearrange("d (a q) -> d a q", q=128),
                                 in_=Win_b[order[1]:order[1] + 1].rearrange("b a d q -> b d a q")), 16)

        for i, p in enumerate(order):
            par = i % 2
            slot = par * 4096
            is_hbc = HBC0 <= p < HBC1
            pm0 = pmA if par == 0 else pmC
            pm1 = pmB if par == 0 else pmD

            @block.tensor
            def _(t, i=i, p=p, slot=slot, pm0=pm0, pm1=pm1, is_hbc=is_hbc, par=par):
                t.wait_ge(s_w, 16 * (i + 1))
                if i == 0:
                    t.wait_ge(s_s, p0_done_s)
                    t.wait_ge(s_v, p0_done_v)
                else:
                    t.wait_ge(s_v, evac_ms[par])
                    t.wait_ge(s_s, evac_ms[(par, "s")])
                for k in range(32):
                    lhs = wsb[:, slot + k * 128: slot + (k + 1) * 128]
                    inc("t", t.matmul(pm0[:, 0:512], lhs, arena[:, k * 1027:k * 1027 + 512],
                                      start=(k == 0), stop=(k == 31)))
                for k in range(32):
                    lhs = wsb[:, slot + k * 128: slot + (k + 1) * 128]
                    inc("t", t.matmul(pm1[:, 0:512], lhs, arena[:, k * 1027 + 512:k * 1027 + 1024],
                                      start=(k == 0), stop=(k == 31)))
                if is_hbc:
                    for k in range(32):
                        lhs = wsb[:, slot + k * 128: slot + (k + 1) * 128]
                        inc("t", t.matmul(pmH[:, par * 4:par * 4 + 3], lhs,
                                          arena[:, k * 1027 + 1024:k * 1027 + 1027],
                                          start=(k == 0), stop=(k == 31)))
            slab_consumed[i] = C["t"]
            mm_ms = C["t"]

            if i + 2 < len(order):
                @block.gpsimd
                def _(g, i=i, par=par):
                    g.wait_ge(s_t, slab_consumed[i])
                    inc("w", g.dma_start(out=wsb[:, par * 4096:par * 4096 + 4096].rearrange("d (a q) -> d a q", q=128),
                                         in_=Win_b[order[i + 2]:order[i + 2] + 1].rearrange("b a d q -> b d a q")), 16)

            if p == DTT:
                @block.vector
                def _(v, pm0=pm0, pm1=pm1, mm_ms=mm_ms):
                    v.wait_ge(s_t, mm_ms)
                    inc("v", v.tensor_copy(dt64[:, 0:512], pm0[0:64, 0:512]))
                    inc("v", v.tensor_copy(dt64[:, 512:1024], pm1[0:64, 0:512]))
                dtv = C["v"]
                evac_ms[par] = C["v"]
                evac_ms[(par, "s")] = C["s"]

                @block.scalar
                def _(s, dtv=dtv):
                    s.wait_ge(s_v, dtv)
                    s.wait_ge(s_l, const_l)
                    inc("s", s.activation(dtsb[:, 0:1024], dt64[:, 0:1024], AF.Abs, bias=dtb_col[:, 0:1]))
                    inc("s", s.activation(dtsb[:, 0:1024], dtsb[:, 0:1024], AF.Exp, scale=-1.0))
                    inc("s", s.activation(dtsb[:, 0:1024], dtsb[:, 0:1024], AF.Ln, bias=1.0))
                    inc("s", s.activation(Acol[:, 0:1], Alog_col[:, 0:1], AF.Exp))
                sps = C["s"]

                @block.vector
                def _(v, sps=sps):
                    v.wait_ge(s_s, sps)
                    inc("v", v.tensor_scalar(dtp[:, 0:1024], dt64[:, 0:1024], dtb_col[:, 0:1], 0.0, ALU.add, ALU.max))
                    inc("v", v.tensor_tensor(dtp[:, 0:1024], dtp[:, 0:1024], dtsb[:, 0:1024], ALU.add))
                    inc("v", v.tensor_scalar(dtsb[:, 0:1024], dtp[:, 0:1024], Acol[:, 0:1], -1.0, ALU.mult, ALU.mult))
                    for c in range(NQ):
                        inc("v", v.tensor_tensor_scan(cum[:, c * Q:(c + 1) * Q], ones64[:, 0:Q],
                                                      dtsb[:, c * Q:(c + 1) * Q], 0.0, ALU.mult, ALU.add))
                cumv0 = C["v"]

                @block.scalar
                def _(s, cumv0=cumv0):
                    s.wait_ge(s_v, cumv0)
                    for c in range(NQ):
                        inc("s", s.copy(ends[:, c:c + 1], cum[:, c * Q + Q - 1:c * Q + Q]))
                endss = C["s"]

                @block.vector
                def _(v, endss=endss):
                    v.wait_ge(s_s, endss)
                    inc("v", v.tensor_tensor_scan(scanned[:, 0:NQ], ones64[:, 0:NQ], ends[:, 0:NQ],
                                                  0.0, ALU.mult, ALU.add))
                    inc("v", v.memset(pref[:, 0:1], 0.0))
                scv = C["v"]

                @block.scalar
                def _(s, scv=scv):
                    s.wait_ge(s_v, scv)
                    inc("s", s.copy(pref[:, 1:NQ], scanned[:, 0:NQ - 1]))
                cumv = C["v"]

                for c in range(NQ):
                    @block.tensor
                    def _(t, c=c, cumv=cumv):
                        t.wait_ge(s_v, cumv)
                        t.wait_ge(s_s, C["s"])
                        inc("t", t.matmul(pmS[:, 0:64], cum[:, c * Q:(c + 1) * Q], identf_s[0:64, 0:64],
                                          start=True, stop=True))
                        inc("t", t.matmul(pmS[:, 64:128], dtp[:, c * Q:(c + 1) * Q], identf_s[0:64, 0:64],
                                          start=True, stop=True))
                    ctm = C["t"]

                    @block.scalar
                    def _(s, c=c, ctm=ctm):
                        s.wait_ge(s_t, ctm)
                        inc("s", s.copy(cumT[:, c * 64:(c + 1) * 64], pmS[:, 0:64]))
                        inc("s", s.copy(dtpT[:, c * 64:(c + 1) * 64], pmS[:, 64:128]))

                @block.sync
                def _(sy, cumv=cumv):
                    sy.wait_ge(s_v, cumv)
                    sy.wait_ge(s_s, C["s"])
                    inc("l", sy.dma_start(out=stage[0:1, 0:512], in_=ends[:, 0:NQ]), 16)
                    inc("l", sy.dma_start(out=stage[0:1, 512:1024], in_=pref[:, 0:NQ]), 16)
                stg = C["l"]

                @block.tensor
                def _(t, stg=stg):
                    t.wait_ge(s_l, stg)
                    t.wait_ge(s_s, C["s"])
                    inc("t", t.matmul(pmS[:, 0:512], onesf[0:1, :], stage[0:1, 0:512], start=True, stop=True))
                rep1 = C["t"]

                @block.vector
                def _(v, rep1=rep1):
                    v.wait_ge(s_t, rep1)
                    inc("v", v.tensor_copy(ends_rep[:, 0:512], pmS[:, 0:512]))
                rep1v = C["v"]

                @block.scalar
                def _(s, rep1v=rep1v):
                    s.wait_ge(s_v, rep1v)
                    inc("s", s.activation(dq_rep[:, 0:512], ends_rep[:, 0:512], AF.Exp))

                @block.tensor
                def _(t, rep1v=rep1v):
                    t.wait_ge(s_v, rep1v)
                    inc("t", t.matmul(pmS[:, 0:512], onesf[0:1, :], stage[0:1, 512:1024], start=True, stop=True))
                rep2 = C["t"]

                @block.vector
                def _(v, rep2=rep2):
                    v.wait_ge(s_t, rep2)
                    inc("v", v.tensor_copy(pref_rep[:, 0:512], pmS[:, 0:512]))
                rep2v = C["v"]

                @block.tensor
                def _(t, rep2v=rep2v):
                    t.wait_ge(s_v, rep2v)
                    inc("t", t.matmul(pmS[:, 0:64], onesf[0:1, :], Drow[0:1, 0:64], start=True, stop=True))
                repD = C["t"]

                @block.vector
                def _(v, repD=repD):
                    v.wait_ge(s_t, repD)
                    inc("v", v.tensor_copy(D_rep[:, 0:64], pmS[:, 0:64]))
                evac_ms[par] = C["v"]
                evac_ms[(par, "s")] = C["s"]
            elif is_hbc:
                jj = p - HBC0

                @block.vector
                def _(v, pm0=pm0, pm1=pm1, mm_ms=mm_ms, jj=jj, par=par):
                    v.wait_ge(s_t, mm_ms)
                    v.wait_ge(s_s, C["s"])
                    inc("v", v.tensor_copy(f32a[:, 0:3], pmH[:, par * 4:par * 4 + 3]))
                    inc("v", v.tensor_copy(f32a[:, 3:515], pm0[:, 0:512]))
                    inc("v", v.tensor_copy(f32a[:, 515:1027], pm1[:, 0:512]))
                    inc("v", v.tensor_scalar(f32a[:, 1027:2051], f32a[:, 0:1024], convw_sb[:, jj * 4:jj * 4 + 1],
                                             convb_sb[:, jj:jj + 1], ALU.mult, ALU.add))
                    inc("v", v.scalar_tensor_tensor(f32a[:, 2051:3075], f32a[:, 1:1025],
                                                    convw_sb[:, jj * 4 + 1:jj * 4 + 2], f32a[:, 1027:2051],
                                                    ALU.mult, ALU.add))
                    inc("v", v.scalar_tensor_tensor(f32a[:, 1027:2051], f32a[:, 2:1026],
                                                    convw_sb[:, jj * 4 + 2:jj * 4 + 3], f32a[:, 2051:3075],
                                                    ALU.mult, ALU.add))
                    inc("v", v.scalar_tensor_tensor(f32a[:, 2051:3075], f32a[:, 3:1027],
                                                    convw_sb[:, jj * 4 + 3:jj * 4 + 4], f32a[:, 1027:2051],
                                                    ALU.mult, ALU.add))
                evac_ms[par] = C["v"]
                cvv = C["v"]

                @block.scalar
                def _(s, jj=jj, cvv=cvv):
                    s.wait_ge(s_v, cvv)
                    if jj < 32:
                        s.wait_ge(s_p, evac_ms.get(("xsp", jj - 1), 0))
                        inc("s", s.activation(xstage[:, 0:1024], f32a[:, 2051:3075], AF.Silu))
                    else:
                        inc("s", s.activation(BC[:, (jj - 32) * 1024:(jj - 31) * 1024], f32a[:, 2051:3075], AF.Silu))
                evac_ms[(par, "s")] = C["s"]
                sil = C["s"]
                if jj < 32:
                    @block.sync
                    def _(sy, jj=jj, sil=sil):
                        sy.wait_ge(s_s, sil)
                        evac_ms[("xsp", jj)] = inc("p", sy.dma_start(out=d_xs[:, jj * 1024:(jj + 1) * 1024],
                                                                     in_=xstage[:, 0:1024]), 16)
            else:
                @block.scalar
                def _(s, pm0=pm0, pm1=pm1, mm_ms=mm_ms, p=p):
                    s.wait_ge(s_t, mm_ms)
                    s.wait_ge(s_v, evac_ms.get(("gsv", p - 1), 0))
                    inc("s", s.activation(silu_t[:, 0:512], pm0[:, 0:512], AF.Silu))
                    inc("s", s.activation(silu_t[:, 512:1024], pm1[:, 0:512], AF.Silu))
                evac_ms[(par, "s")] = C["s"]
                gss = C["s"]

                @block.vector
                def _(v, p=p, gss=gss):
                    v.wait_ge(s_s, gss)
                    v.wait_ge(s_p, evac_ms.get(("gsp", p - 1), 0))
                    inc("v", v.tensor_scalar(gstage[:, 0:1024], silu_t[:, 0:1024], nw_sb[:, p:p + 1], None, ALU.mult))
                evac_ms[par] = C["v"]
                evac_ms[("gsv", p)] = C["v"]
                gsv = C["v"]

                @block.sync
                def _(sy, p=p, gsv=gsv):
                    sy.wait_ge(s_v, gsv)
                    evac_ms[("gsp", p)] = inc("p", sy.dma_start(out=d_gate[:, p * 1024:(p + 1) * 1024],
                                                                in_=gstage[:, 0:1024]), 16)

        p1_t, p1_v, p1_s, p1_p = C["t"], C["v"], C["s"], C["p"]

        seq = []
        for j in range(32):
            seq += [j, j + 32]

        @block.gpsimd
        def _(g):
            g.wait_ge(s_t, p1_t)
            inc("w", g.dma_start(out=wsb[:, 0:4096].rearrange("d (a q) -> d a q", q=128),
                                 in_=W12_b[seq[0]:seq[0] + 1].rearrange("b a d q -> b d a q")), 16)
            inc("w", g.dma_start(out=wsb[:, 4096:8192].rearrange("d (a q) -> d a q", q=128),
                                 in_=W12_b[seq[1]:seq[1] + 1].rearrange("b a d q -> b d a q")), 16)
        p1_w_units = C["w"] // 16

        ff_evac = {0: p1_v, (0, "s"): p1_s}
        for jp in range(32):
            @block.tensor
            def _(t, jp=jp):
                t.wait_ge(s_w, 16 * (p1_w_units - 2 + 2 * jp + 1))
                t.wait_ge(s_v, ff_evac[0])
                t.wait_ge(s_s, ff_evac[(0, "s")])
                for k in range(32):
                    lhs = wsb[:, k * 128:(k + 1) * 128]
                    inc("t", t.matmul(pmA[:, 0:512], lhs, arena[:, k * 1027:k * 1027 + 512],
                                      start=(k == 0), stop=(k == 31)))
                for k in range(32):
                    lhs = wsb[:, k * 128:(k + 1) * 128]
                    inc("t", t.matmul(pmB[:, 0:512], lhs, arena[:, k * 1027 + 512:k * 1027 + 1024],
                                      start=(k == 0), stop=(k == 31)))
            x1ms = C["t"]

            if jp + 1 < 32:
                @block.gpsimd
                def _(g, jp=jp, x1ms=x1ms):
                    g.wait_ge(s_t, x1ms)
                    inc("w", g.dma_start(out=wsb[:, 0:4096].rearrange("d (a q) -> d a q", q=128),
                                         in_=W12_b[seq[2 * jp + 2]:seq[2 * jp + 2] + 1].rearrange("b a d q -> b d a q")), 16)

            @block.tensor
            def _(t, jp=jp):
                t.wait_ge(s_w, 16 * (p1_w_units - 2 + 2 * jp + 2))
                for k in range(32):
                    lhs = wsb[:, 4096 + k * 128:4096 + (k + 1) * 128]
                    inc("t", t.matmul(pmC[:, 0:512], lhs, arena[:, k * 1027:k * 1027 + 512],
                                      start=(k == 0), stop=(k == 31)))
                for k in range(32):
                    lhs = wsb[:, 4096 + k * 128:4096 + (k + 1) * 128]
                    inc("t", t.matmul(pmD[:, 0:512], lhs, arena[:, k * 1027 + 512:k * 1027 + 1024],
                                      start=(k == 0), stop=(k == 31)))
            x2ms = C["t"]

            if jp + 1 < 32:
                @block.gpsimd
                def _(g, jp=jp, x2ms=x2ms):
                    g.wait_ge(s_t, x2ms)
                    inc("w", g.dma_start(out=wsb[:, 4096:8192].rearrange("d (a q) -> d a q", q=128),
                                         in_=W12_b[seq[2 * jp + 3]:seq[2 * jp + 3] + 1].rearrange("b a d q -> b d a q")), 16)

            @block.scalar
            def _(s, jp=jp, x1ms=x1ms):
                s.wait_ge(s_t, x1ms)
                s.wait_ge(s_v, ff_evac[0])
                inc("s", s.activation(silu_t[:, 0:512], pmA[:, 0:512], AF.Silu, bias=b12_sb[:, jp:jp + 1]))
                inc("s", s.activation(silu_t[:, 512:1024], pmB[:, 0:512], AF.Silu, bias=b12_sb[:, jp:jp + 1]))
            sls = C["s"]

            @block.vector
            def _(v, jp=jp, x2ms=x2ms, sls=sls):
                v.wait_ge(s_t, x2ms)
                v.wait_ge(s_s, sls)
                v.wait_ge(s_p, ff_evac.get(("x4sp", jp - 1), 0))
                inc("v", v.scalar_tensor_tensor(x4stage[:, 0:512], pmC[:, 0:512], b12_sb[:, 32 + jp:33 + jp],
                                                silu_t[:, 0:512], ALU.add, ALU.mult))
                inc("v", v.scalar_tensor_tensor(x4stage[:, 512:1024], pmD[:, 0:512], b12_sb[:, 32 + jp:33 + jp],
                                                silu_t[:, 512:1024], ALU.add, ALU.mult))
            ff_evac[0] = C["v"]
            ff_evac[(0, "s")] = C["s"]
            x4v = C["v"]

            @block.sync
            def _(sy, jp=jp, x4v=x4v):
                sy.wait_ge(s_v, x4v)
                ff_evac[("x4sp", jp)] = inc("p", sy.dma_start(out=d_x4[:, jp * 1024:(jp + 1) * 1024],
                                                              in_=x4stage[:, 0:1024]), 16)

        p1b_t, p1b_v, p1b_s = C["t"], C["v"], C["s"]

        @block.vector
        def _(v):
            v.wait_ge(s_t, p1b_t)
            inc("v", v.memset(srun[:, :], 0.0))
            inc("v", v.memset(sbf[:, :], 0.0))

        for c in range(NQ):
            @block.sync
            def _(sy, c=c):
                sy.wait_ge(s_t, C["t"])
                sy.wait_ge(s_p, p1_p)
                inc("l", sy.dma_start(out=wsb[:, 0:4096].rearrange("p (k t) -> p k t", t=128),
                                      in_=d_xs.rearrange("p (k u) -> p k u", u=1024)[:, :, c * 128:(c + 1) * 128]), 16)
            xchl = C["l"]

            for h in range(H):
                g = h // 8
                par = h % 2
                if h % 8 == 0:
                    @block.tensor
                    def _(t, g=g, c=c, xchl=xchl):
                        t.wait_ge(s_v, C["v"])
                        t.wait_ge(s_s, C["s"])
                        if h == 0 and g == 0:
                            t.wait_ge(s_l, xchl)
                        inc("t", t.matmul(pmS[:, 0:128], BC[:, g * 1024 + c * 128:g * 1024 + (c + 1) * 128],
                                          BC[:, (8 + g) * 1024 + c * 128:(8 + g) * 1024 + (c + 1) * 128],
                                          start=True, stop=True))
                        inc("t", t.transpose(pbT[:, 0:128], BC[:, g * 1024 + c * 128:g * 1024 + (c + 1) * 128],
                                             identb_s[:, :]))
                    gms = C["t"]

                    @block.scalar
                    def _(s, gms=gms):
                        s.wait_ge(s_t, gms)
                        inc("s", s.copy(G_bf[:, :], pmS[:, 0:128]))
                        inc("s", s.copy(BT_sb[:, :], pbT[:, 0:128]))

                if h % 4 == 0:
                    spar = (h // 4) % 2

                    @block.sync
                    def _(sy, h=h, c=c, spar=spar):
                        sy.wait_ge(s_t, C["t"])
                        inc("l", sy.dma_start(
                            out=stage[0:1, spar * 512:spar * 512 + 512],
                            in_=cum[h:h + 4, c * Q:(c + 1) * Q]), 16)
                    stl = C["l"]

                    @block.tensor
                    def _(t, spar=spar, stl=stl, xchl=xchl):
                        t.wait_ge(s_l, max(stl, xchl))
                        t.wait_ge(s_v, C["v"])
                        t.wait_ge(s_s, C["s"])
                        inc("t", t.matmul(pmA[:, 0:512], onesf[0:1, :], stage[0:1, spar * 512:spar * 512 + 512],
                                          start=True, stop=True))
                    ems = C["t"]

                    @block.vector
                    def _(v, h=h, c=c, ems=ems):
                        v.wait_ge(s_t, ems)
                        for q in range(4):
                            inc("v", v.scalar_tensor_tensor(e_sb[:, q * 128:(q + 1) * 128], pmA[:, q * 128:(q + 1) * 128],
                                                            cumT[:, c * 64 + h + q:c * 64 + h + q + 1], AM[:, :],
                                                            ALU.subtract, ALU.add))
                    ev = C["v"]

                    @block.scalar
                    def _(s, ev=ev):
                        s.wait_ge(s_v, ev)
                        inc("s", s.activation(Dm[:, 0:512], e_sb[:, 0:512], AF.Exp))

                @block.vector
                def _(v, h=h):
                    v.wait_ge(s_s, C["s"])
                    inc("v", v.tensor_tensor(Mp[:, (h % 2) * 128:(h % 2) * 128 + 128],
                                             Dm[:, (h % 4) * 128:(h % 4) * 128 + 128], G_bf[:, :], ALU.mult))
                mpv = C["v"]

                @block.tensor
                def _(t, h=h, par=par):
                    t.wait_ge(s_s, C["s"])
                    bb = (h % 2) * 64
                    inc("t", t.transpose(pbT[:, 128 + par * 64:128 + par * 64 + 64],
                                         wsb[bb:bb + 64, (h // 2) * 128:(h // 2) * 128 + 128],
                                         identb_s[bb:bb + 64, bb:bb + 64]))
                xtm = C["t"]

                @block.scalar
                def _(s, h=h, par=par, c=c, xtm=xtm):
                    s.wait_ge(s_t, xtm)
                    s.wait_ge(s_v, C["v"])
                    inc("s", s.copy(xts[:, par * 64:par * 64 + 64], pbT[:, 128 + par * 64:128 + par * 64 + 64]))
                    inc("s", s.activation(colb[:, par * 2:par * 2 + 1], cumT[:, c * 64 + h:c * 64 + h + 1], AF.Exp))
                    inc("s", s.activation(colb[:, par * 2 + 1:par * 2 + 2], cumT[:, c * 64 + h:c * 64 + h + 1],
                                          AF.Exp, scale=-1.0, bias=ends_rep[:, h * 8 + c:h * 8 + c + 1]))
                xtc = C["s"]

                @block.vector
                def _(v, h=h, par=par, c=c, xtc=xtc):
                    v.wait_ge(s_s, xtc)
                    inc("v", v.tensor_scalar(dtx[:, par * 64:par * 64 + 64], xts[:, par * 64:par * 64 + 64],
                                             dtpT[:, c * 64 + h:c * 64 + h + 1], None, ALU.mult))
                    inc("v", v.tensor_scalar(dtxs[:, par * 64:par * 64 + 64], dtx[:, par * 64:par * 64 + 64],
                                             colb[:, par * 2 + 1:par * 2 + 2], None, ALU.mult))
                dxv = C["v"]

                @block.tensor
                def _(t, h=h, g=g, c=c, par=par, dxv=dxv):
                    t.wait_ge(s_v, dxv)
                    inc("t", t.matmul(pmC[:, par * 64:par * 64 + 64], Mp[:, (h % 2) * 128:(h % 2) * 128 + 128],
                                      dtx[:, par * 64:par * 64 + 64], start=True, stop=True))
                    inc("t", t.matmul(pmC[:, 128 + par * 64:128 + par * 64 + 64],
                                      BC[:, (8 + g) * 1024 + c * 128:(8 + g) * 1024 + (c + 1) * 128],
                                      sbf[:, h * 64:(h + 1) * 64], start=True, stop=True))
                    inc("t", t.matmul(pmC[:, 256 + par * 64:256 + par * 64 + 64], BT_sb[:, :],
                                      dtxs[:, par * 64:par * 64 + 64], start=True, stop=True))
                yms = C["t"]

                @block.vector
                def _(v, h=h, c=c, par=par, yms=yms):
                    v.wait_ge(s_t, yms)
                    inc("v", v.tensor_scalar(yg1[:, par * 64:par * 64 + 64], pmC[:, 128 + par * 64:128 + par * 64 + 64],
                                             colb[:, par * 2:par * 2 + 1], None, ALU.mult))
                    inc("v", v.scalar_tensor_tensor(yg2[:, par * 64:par * 64 + 64], xts[:, par * 64:par * 64 + 64],
                                                    D_rep[:, h:h + 1], pmC[:, par * 64:par * 64 + 64],
                                                    ALU.mult, ALU.add))
                    inc("v", v.tensor_tensor(arena[:, c * 4096 + h * 64:c * 4096 + (h + 1) * 64],
                                             yg1[:, par * 64:par * 64 + 64], yg2[:, par * 64:par * 64 + 64], ALU.add))
                    inc("v", v.tensor_scalar(yg1[:, par * 64:par * 64 + 64], srun[:, h * 64:(h + 1) * 64],
                                             dq_rep[:, h * 8 + c:h * 8 + c + 1], None, ALU.mult))
                    inc("v", v.tensor_tensor(srun[:, h * 64:(h + 1) * 64], yg1[:, par * 64:par * 64 + 64],
                                             pmC[:, 256 + par * 64:256 + par * 64 + 64], ALU.add))

            @block.scalar
            def _(s, c=c):
                s.wait_ge(s_v, C["v"])
                inc("s", s.copy(sbf[:, :], srun[:, :]))

        p2_t, p2_v, p2_s = C["t"], C["v"], C["s"]

        @block.sync
        def _(sy):
            sy.wait_ge(s_s, p2_s)
            sy.wait_ge(s_v, p2_v)
            inc("p", sy.dma_start(out=cc_in.ap()[:, 0:4096], in_=srun[:, :]), 16)
            inc("p", sy.dma_start(out=cc_in.ap()[0:64, 4096:4097], in_=scanned[:, NQ - 1:NQ],
                                  allow_slow_non_contiguous=True), 16)
        blobp = C["p"]

        @block.gpsimd
        def _(g):
            g.wait_ge(s_p, blobp)
            inc("cc", g.collective_compute("AllGather", ALU.bypass, replica_groups=[list(range(NCORE))],
                                           ins=[cc_in.ap()], outs=[cc_out.ap()]))

        @block.sync
        def _(sy):
            sy.wait_ge(s_cc, C["cc"])
            for cp in range(NCORE):
                inc("l", sy.dma_start(out=CEall[:, cp:cp + 1], in_=cc_out.ap()[cp * 128:cp * 128 + 64, 4096:4097],
                                      allow_slow_non_contiguous=True), 16)
        cel = C["l"]

        @block.vector
        def _(v, cel=cel):
            v.wait_ge(s_l, cel)
            inc("v", v.memset(srun[:, :], 0.0))
        w8v = C["v"]

        @block.sync
        def _(sy, cel=cel, w8v=w8v):
            sy.wait_ge(s_l, cel)
            sy.wait_ge(s_t, C["t"])
            inc("l", sy.dma_start(out=stage[0:1, 0:512], in_=CEall[:, 0:NCORE]), 16)
        w8l = C["l"]

        @block.tensor
        def _(t, w8l=w8l):
            t.wait_ge(s_l, w8l)
            t.wait_ge(s_v, C["v"])
            t.wait_ge(s_s, C["s"])
            inc("t", t.matmul(pmS[:, 0:512], onesf[0:1, :], stage[0:1, 0:512], start=True, stop=True))
        w8m = C["t"]

        @block.vector
        def _(v, w8m=w8m):
            v.wait_ge(s_t, w8m)
            inc("v", v.tensor_tensor(W8_rep[:, 0:512], pmS[:, 0:512], M1sb[:, 0:512], ALU.mult))
        w8vb = C["v"]

        @block.scalar
        def _(s, w8vb=w8vb):
            s.wait_ge(s_v, w8vb)
            inc("s", s.activation(ends_rep[:, 0:512], W8_rep[:, 0:512], AF.Exp))
        w8s = C["s"]

        for cp in range(NCORE):
            @block.sync
            def _(sy, cp=cp, w8s=w8s):
                sy.wait_ge(s_v, C["v"])
                sy.wait_ge(s_s, w8s)
                inc("l", sy.dma_start(out=f32a[:, :], in_=cc_out.ap()[cp * 128:(cp + 1) * 128, 0:4096]), 16)
            sgl = C["l"]

            @block.vector
            def _(v, cp=cp, sgl=sgl):
                v.wait_ge(s_l, sgl)
                v.wait_ge(s_s, C["s"])
                for h in range(H):
                    inc("v", v.tensor_scalar(srun[:, h * 64:(h + 1) * 64], srun[:, h * 64:(h + 1) * 64],
                                             ends_rep[:, h * 8 + cp:h * 8 + cp + 1], None, ALU.mult))
                    inc("v", v.tensor_scalar(yg1[:, 0:64], f32a[:, h * 64:(h + 1) * 64],
                                             M1sb[:, h * 8 + cp:h * 8 + cp + 1], None, ALU.mult))
                    inc("v", v.tensor_tensor(srun[:, h * 64:(h + 1) * 64], srun[:, h * 64:(h + 1) * 64],
                                             yg1[:, 0:64], ALU.add))
        sinv = C["v"]

        @block.scalar
        def _(s, sinv=sinv):
            s.wait_ge(s_v, sinv)
            inc("s", s.copy(sbf[:, :], srun[:, :]))
        sinbf = C["s"]

        cms = {"ecs": {}, "ycm": {}, "cadd": {0: 0, 1: 0}}

        def emit_ecs(ii, cms=cms):
            c, hh = divmod(ii, H)
            par = hh % 2

            @block.scalar
            def _(s, c=c, hh=hh, par=par):
                s.wait_ge(s_s, sinbf)
                s.wait_ge(s_v, cms["cadd"][par])
                inc("s", s.activation(colb[:, 4 + par:5 + par], cumT[:, c * 64 + hh:c * 64 + hh + 1], AF.Exp,
                                      bias=pref_rep[:, hh * 8 + c:hh * 8 + c + 1]))
            cms["ecs"][ii] = C["s"]

        def emit_ycm(ii, cms=cms):
            c, hh = divmod(ii, H)
            g = hh // 8
            par = hh % 2

            @block.tensor
            def _(t, c=c, hh=hh, g=g, par=par):
                t.wait_ge(s_s, cms["ecs"][ii])
                t.wait_ge(s_v, cms["cadd"][par])
                inc("t", t.matmul(pmC[:, par * 64:par * 64 + 64],
                                  BC[:, (8 + g) * 1024 + c * 128:(8 + g) * 1024 + (c + 1) * 128],
                                  sbf[:, hh * 64:(hh + 1) * 64], start=True, stop=True))
            cms["ycm"][ii] = C["t"]

        def emit_cadd(ii, cms=cms):
            c, hh = divmod(ii, H)
            par = hh % 2

            @block.vector
            def _(v, c=c, hh=hh, par=par):
                v.wait_ge(s_t, cms["ycm"][ii])
                inc("v", v.tensor_scalar(yg1[:, par * 64:par * 64 + 64], pmC[:, par * 64:par * 64 + 64],
                                         colb[:, 4 + par:5 + par], None, ALU.mult))
                inc("v", v.tensor_tensor(arena[:, c * 4096 + hh * 64:c * 4096 + (hh + 1) * 64],
                                         arena[:, c * 4096 + hh * 64:c * 4096 + (hh + 1) * 64],
                                         yg1[:, par * 64:par * 64 + 64], ALU.add))
            cms["cadd"][par] = C["v"]

        NTOT = NQ * H
        emit_ecs(0)
        for ii in range(NTOT):
            if ii > 0:
                emit_cadd(ii - 1)
            if ii + 1 < NTOT:
                emit_ecs(ii + 1)
            emit_ycm(ii)
        emit_cadd(NTOT - 1)

        p4_t, p4_v, p4_s = C["t"], C["v"], C["s"]

        for c in range(NQ):
            for g in range(GD):
                @block.scalar
                def _(s, c=c, g=g):
                    if c == 0 and g == 0:
                        s.wait_ge(s_v, p4_v)
                    inc("s", s.activation(f32a[:, 0:512], arena[:, c * 4096 + g * 512:c * 4096 + (g + 1) * 512],
                                          AF.Square, accum_out=ss_sb[:, c * 8 + g:c * 8 + g + 1]))
        sqs = C["s"]

        @block.vector
        def _(v, sqs=sqs):
            v.wait_ge(s_s, sqs)
            inc("v", v.tensor_scalar(rg_sb[:, 0:64], ss_sb[:, 0:64], 1.0 / 512.0, 1e-5, ALU.mult, ALU.add))
        rgv = C["v"]

        @block.scalar
        def _(s, rgv=rgv):
            s.wait_ge(s_v, rgv)
            inc("s", s.activation(rg_sb[:, 0:64], rg_sb[:, 0:64], AF.Ln))
            inc("s", s.activation(rg_sb[:, 0:64], rg_sb[:, 0:64], AF.Exp, scale=-0.5))
        rgs = C["s"]

        @block.vector
        def _(v, rgs=rgs):
            v.wait_ge(s_s, rgs)
            for c in range(NQ):
                for g in range(GD):
                    inc("v", v.tensor_scalar(arena[:, c * 4096 + g * 512:c * 4096 + (g + 1) * 512],
                                             arena[:, c * 4096 + g * 512:c * 4096 + (g + 1) * 512],
                                             rg_sb[:, c * 8 + g:c * 8 + g + 1], None, ALU.mult))
        rnv = C["v"]

        last_yn = {0: 0, 1: 0}
        for khalf in range(2):
            @block.sync
            def _(sy, khalf=khalf, rnv=rnv):
                sy.wait_ge(s_v, rnv)
                sy.wait_ge(s_t, C["t"])
                sy.wait_ge(s_p, p1_p)
                inc("l", sy.dma_start(out=BC[:, 0:16384], in_=d_gate[:, khalf * 16384:(khalf + 1) * 16384]), 16)
            gl = C["l"]

            for k in range(khalf * 16, khalf * 16 + 16):
                for c in range(NQ):
                    par = c % 2

                    @block.tensor
                    def _(t, k=k, c=c, par=par, gl=gl):
                        t.wait_ge(s_l, gl)
                        t.wait_ge(s_v, C["v"])
                        inc("t", t.transpose(pbT[:, par * 128:par * 128 + 128],
                                             arena[:, c * 4096 + k * 128:c * 4096 + (k + 1) * 128],
                                             identb_s[:, :]))
                    ytm = C["t"]

                    @block.vector
                    def _(v, k=k, c=c, par=par, khalf=khalf, ytm=ytm):
                        v.wait_ge(s_t, ytm)
                        v.wait_ge(s_p, last_yn[par])
                        inc("v", v.tensor_tensor(ynstage[:, par * 128:par * 128 + 128],
                                                 pbT[:, par * 128:par * 128 + 128],
                                                 BC[:, (k - khalf * 16) * 1024 + c * 128:(k - khalf * 16) * 1024 + (c + 1) * 128],
                                                 ALU.mult))
                    ynv = C["v"]

                    @block.sync
                    def _(sy, k=k, c=c, par=par, ynv=ynv):
                        sy.wait_ge(s_v, ynv)
                        last_yn[par] = inc("p", sy.dma_start(out=d_yn[:, k * 1024 + c * 128:k * 1024 + (c + 1) * 128],
                                                             in_=ynstage[:, par * 128:par * 128 + 128]), 16)

        for tk in range(2):
            @block.sync
            def _(sy, tk=tk):
                sy.wait_ge(s_p, C["p"])
                sy.wait_ge(s_t, C["t"])
                sy.wait_ge(s_s, C["s"])
                inc("l", sy.dma_start(out=arena[:, 0:16384].rearrange("p (k t) -> p k t", t=512),
                                      in_=d_yn.rearrange("p (k u) -> p k u", u=1024)[:, :, tk * 512:tk * 512 + 512]), 16)
                inc("l", sy.dma_start(out=arena[:, 16384:32768].rearrange("p (k t) -> p k t", t=512),
                                      in_=d_x4.rearrange("p (k u) -> p k u", u=1024)[:, :, tk * 512:tk * 512 + 512]), 16)
            rhl = C["l"]

            slabbuf = [wsb, BC]
            w_units = {}

            @block.gpsimd
            def _(g, tk=tk):
                g.wait_ge(s_t, C["t"])
                g.wait_ge(s_v, C["v"])
                inc("w", g.dma_start(out=wsb[:, 0:4096].rearrange("d (a q) -> d a q", q=128),
                                     in_=Wout_b[0:1].rearrange("b a d q -> b d a q")), 16)
                inc("w", g.dma_start(out=wsb[:, 4096:8192].rearrange("d (a q) -> d a q", q=128),
                                     in_=W3_b[0:1].rearrange("b a d q -> b d a q")), 16)
                inc("w", g.dma_start(out=BC[:, 0:4096].rearrange("d (a q) -> d a q", q=128),
                                     in_=Wout_b[1:2].rearrange("b a d q -> b d a q")), 16)
                inc("w", g.dma_start(out=BC[:, 4096:8192].rearrange("d (a q) -> d a q", q=128),
                                     in_=W3_b[1:2].rearrange("b a d q -> b d a q")), 16)
            w_units[0] = C["w"] - 32
            w_units[1] = C["w"]
            oms_hist = {}

            o_evac = {0: C["v"], 1: C["v"]}
            for do in range(32):
                par = do % 2
                pmo = pmA if par == 0 else pmB
                sb = slabbuf[par]

                @block.tensor
                def _(t, do=do, pmo=pmo, par=par, rhl=rhl, sb=sb):
                    t.wait_ge(s_l, rhl)
                    t.wait_ge(s_w, w_units[do])
                    t.wait_ge(s_v, o_evac[par])
                    for k in range(32):
                        inc("t", t.matmul(pmo[:, 0:512], sb[:, k * 128:(k + 1) * 128],
                                          arena[:, k * 512:(k + 1) * 512], start=(k == 0), stop=False))
                    for k in range(32):
                        inc("t", t.matmul(pmo[:, 0:512], sb[:, 4096 + k * 128:4096 + (k + 1) * 128],
                                          arena[:, 16384 + k * 512:16384 + (k + 1) * 512],
                                          start=False, stop=(k == 31)))
                oms = C["t"]
                oms_hist[do] = oms

                if do + 2 < 32:
                    @block.gpsimd
                    def _(g, do=do, par=par, sb=sb):
                        g.wait_ge(s_t, oms_hist[do])
                        inc("w", g.dma_start(out=sb[:, 0:4096].rearrange("d (a q) -> d a q", q=128),
                                             in_=Wout_b[do + 2:do + 3].rearrange("b a d q -> b d a q")), 16)
                        inc("w", g.dma_start(out=sb[:, 4096:8192].rearrange("d (a q) -> d a q", q=128),
                                             in_=W3_b[do + 2:do + 3].rearrange("b a d q -> b d a q")), 16)
                    w_units[do + 2] = C["w"]

                @block.sync
                def _(sy, do=do, tk=tk, par=par):
                    sy.wait_ge(s_v, o_evac.get((par, "xt"), 0))
                    inc("l", sy.dma_start(out=f32a[:, par * 512:par * 512 + 512],
                                          in_=xT_loc[do * 128:(do + 1) * 128, tk * 512:tk * 512 + 512]), 16)
                xtl = C["l"]

                @block.vector
                def _(v, do=do, pmo=pmo, par=par, oms=oms, xtl=xtl):
                    v.wait_ge(s_t, oms)
                    v.wait_ge(s_l, xtl)
                    v.wait_ge(s_o, o_evac.get((par, "op"), 0))
                    inc("v", v.scalar_tensor_tensor(f32a[:, 2048 + par * 512:2048 + par * 512 + 512],
                                                    pmo[:, 0:512], b3_sb[:, do:do + 1],
                                                    f32a[:, par * 512:par * 512 + 512], ALU.add, ALU.add))
                o_evac[par] = C["v"]
                o_evac[(par, "xt")] = C["v"]
                ov = C["v"]

                @block.sync
                def _(sy, do=do, tk=tk, par=par, ov=ov):
                    sy.wait_ge(s_v, ov)
                    o_evac[(par, "op")] = inc("o", sy.dma_start(
                        out=outT[do * 128:(do + 1) * 128, tk * 512:tk * 512 + 512],
                        in_=f32a[:, 2048 + par * 512:2048 + par * 512 + 512]), 16)

        @block.sync
        def _(sy):
            sy.wait_ge(s_o, C["o"])

    return nc


_NC_CACHE = [None]


def _host_prep(inp):
    f32 = np.float32
    x = np.asarray(inp["L0_attn_hidden_states"], f32).reshape(L, DM)
    Win = np.asarray(inp["L0_attn_in_proj_w"], f32)
    Wout = np.asarray(inp["L0_attn_out_proj_w"], f32)
    W12 = np.asarray(inp["L0_ffn_w12"], f32)
    W3 = np.asarray(inp["L0_ffn_w3"], f32)

    def blockT(W, npt):
        out_dim = W.shape[0]
        pad = npt * 128 - out_dim
        if pad:
            W = np.concatenate([W, np.zeros((pad, W.shape[1]), f32)], axis=0)
        WT = W.T
        B = WT.reshape(32, 128, npt, 128).transpose(2, 0, 1, 3)
        return np.ascontiguousarray(B)

    Win_b = blockT(Win, NPT)
    Wout_b = blockT(Wout, 32)
    W12_b = blockT(W12, 64)
    W3_b = blockT(W3, 32)
    xT = np.ascontiguousarray(x.T)

    identb = np.eye(128, dtype=ml_dtypes.bfloat16)
    identf = np.eye(128, dtype=f32)
    t_i = np.arange(128)
    AM = np.where(t_i[None, :] >= t_i[:, None], 0.0, -1e30).astype(f32)

    maps = []
    for c in range(NCORE):
        t0 = c * LC
        halo = x[t0 - 3:t0] if c > 0 else np.zeros((3, DM), f32)
        m1row = np.zeros((1, H * NCORE), f32)
        for hh in range(H):
            for jj in range(NCORE):
                if jj < c:
                    m1row[0, hh * NCORE + jj] = 1.0
        M1full = np.ascontiguousarray(np.repeat(m1row, 128, axis=0))
        maps.append({
            "x_loc": np.ascontiguousarray(x[t0:t0 + LC]),
            "x_halo": np.ascontiguousarray(halo),
            "xT_loc": np.ascontiguousarray(xT[:, t0:t0 + LC]),
            "Win_b": Win_b, "Wout_b": Wout_b, "W12_b": W12_b, "W3_b": W3_b,
            "conv_w": np.asarray(inp["L0_attn_conv_weight"], f32),
            "conv_b": np.asarray(inp["L0_attn_conv_bias"], f32),
            "dt_bias": np.asarray(inp["L0_attn_dt_bias"], f32),
            "A_log": np.asarray(inp["L0_attn_A_log"], f32),
            "D_in": np.asarray(inp["L0_attn_D"], f32),
            "nw_in": np.asarray(inp["L0_attn_norm_weight"], f32),
            "b12_in": np.asarray(inp["L0_ffn_b12"], f32),
            "b3_in": np.asarray(inp["L0_ffn_b3"], f32),
            "identb": identb, "identf": identf, "AM_in": AM,
            "M1_in": M1full,
        })
    return maps


def kernel(**inp):
    maps = _host_prep(inp)
    if _NC_CACHE[0] is None:
        _NC_CACHE[0] = build_nc()
    nc = _NC_CACHE[0]
    res = run_bass_kernel_spmd(nc, maps, core_ids=list(range(NCORE)))
    outs = [np.asarray(res.results[c]["outT"]) for c in range(NCORE)]
    outT = np.concatenate(outs, axis=1)
    return np.ascontiguousarray(outT.T).reshape(1, L, DM).astype(np.float32)


if __name__ == "__main__":
    import time
    t0 = time.time()
    nc = build_nc()
    print("build ok", time.time() - t0, "s")
